# version 24 (final)
# baseline (speedup 1.0000x reference)
import numpy as np
from contextlib import ExitStack

import concourse.bass as bass
import concourse.mybir as mybir
from concourse.bass_utils import run_bass_kernel_spmd

F32 = mybir.dt.float32
BF16 = mybir.dt.bfloat16
I8 = mybir.dt.int8
AL = mybir.AluOpType
AF = mybir.ActivationFunctionType
AX = mybir.AxisListType.X

DIM = 1024
DIN = 2048
NH = 32
T = 2048
ST = 2176
NT = 8
TT = 256
TW = TT + 3
Q = 128
EPS = 1e-5
CONSTVALS = (1e-5, 1.0, 0.0, -1.0, -0.5, 1.0 / 2048.0, 1e-30, 0.5, 12582912.0, -12582912.0)

ENGS = ("sync", "vector", "scalar", "tensor", "gpsimd")
ESEM = {"sync": "sy", "vector": "dve", "scalar": "act", "tensor": "pe", "gpsimd": "pool"}
DSEM = {"sync": "dS", "vector": "dV", "scalar": "dA", "tensor": "dT", "gpsimd": "dG"}


class Prog:

    def __init__(self, sems):
        self.sems = sems
        self.steps = {e: [] for e in ENGS}
        self.cnt = {s: 0 for s in sems}
        self.water = {e: {s: 0 for s in sems} for e in ENGS}
        self.last_w = {}
        self.readers = {}
        self.dmaidx = {q: 0 for q in ("dS", "dV", "dA", "dG")}

    ELIDE = True
    ELIDE_ENGS = ("scalar", "tensor")

    def _waits(self, eng, reads, writes):
        own = ESEM[eng] if (Prog.ELIDE and eng in Prog.ELIDE_ENGS) else "-"
        need = {}
        for r in tuple(reads) + tuple(writes):
            lw = self.last_w.get(r)
            if lw and lw[0] != own and self.water[eng].get(lw[0], 0) < lw[1]:
                need[lw[0]] = max(need.get(lw[0], 0), lw[1])
        for r in writes:
            for s, v in self.readers.get(r, ()):
                if s != own and self.water[eng].get(s, 0) < v:
                    need[s] = max(need.get(s, 0), v)
        return sorted(need.items())

    def _book(self, eng, waits, semname, val, reads, writes):
        for s, v in waits:
            self.water[eng][s] = max(self.water[eng][s], v)
        for r in writes:
            self.last_w[r] = (semname, val)
            self.readers[r] = []
        for r in reads:
            self.readers.setdefault(r, []).append((semname, val))

    SERIAL_DMA = False
    DMA_W = 8

    def op(self, eng, fn, reads=(), writes=(), dma=False, sem=None):
        waits = self._waits(eng, reads, writes)
        if dma:
            q = DSEM[eng]
            slot = self.dmaidx[q] % self.DMA_W
            self.dmaidx[q] += 1
            semname = f"{q}{slot}"
            amt = 16
            if self.SERIAL_DMA:
                prevq = f"{q}{(slot - 1) % self.DMA_W}"
                w = dict(waits)
                if self.cnt.get(prevq, 0) > 0:
                    w[prevq] = max(w.get(prevq, 0), self.cnt[prevq])
                waits = sorted(w.items())
            if self.cnt[semname] > 0:
                w = dict(waits)
                if self.water[eng].get(semname, 0) < self.cnt[semname]:
                    w[semname] = max(w.get(semname, 0), self.cnt[semname])
                waits = sorted(w.items())
        else:
            semname = sem or ESEM[eng]
            amt = 1
        self.cnt[semname] += amt
        val = self.cnt[semname]
        sems = self.sems

        def emit(e, fn=fn, waits=waits, semname=semname, amt=amt):
            for s, v in waits:
                e.wait_ge(sems[s], v)
            fn(e).then_inc(sems[semname], amt)

        self.steps[eng].append(emit)
        self._book(eng, waits, semname, val, reads, writes)

    def mms(self, fns, reads=(), writes=()):
        waits = self._waits("tensor", reads, writes)
        self.cnt["pe"] += 1
        val = self.cnt["pe"]
        sems = self.sems

        def emit(e, fns=fns, waits=waits):
            for s, v in waits:
                e.wait_ge(sems[s], v)
            last = None
            for f in fns:
                last = f(e)
            last.then_inc(sems["pe"], 1)

        self.steps["tensor"].append(emit)
        self._book("tensor", waits, "pe", val, reads, writes)

    def final_barrier(self, eng):
        sems = self.sems
        vals = [(s, v) for s, v in self.cnt.items() if v > 0]

        def emit(e, vals=vals):
            for s, v in vals:
                e.wait_ge(sems[s], v)

        self.steps[eng].append(emit)


def rep_h(ap2d, nh, nw):
    return ap2d.unsqueeze(2).broadcast_to([ap2d.shape[0], nh, nw])


def rep_t(ap2d, nh):
    return ap2d.unsqueeze(1).broadcast_to([ap2d.shape[0], nh, ap2d.shape[1]])


def build_nc(a_tiles=NT, c_tiles=NT, dbg=(), detect_races=False):
    nc = bass.Bass("TRN2", target_bir_lowering=False, debug=False,
                   detect_race_conditions=detect_races)

    din = {}
    def I(name, shape, dtype=F32):
        din[name] = nc.dram_tensor(name, shape, dtype, kind="ExternalInput").ap()
        return din[name]

    Xc = I("Xc", [ST, DIM])
    WinT = I("WinT", [DIM, 4256])
    WoT = I("WoT", [DIN, DIM])
    WkT = I("WkT", [DIM, DIM])
    Wq8 = I("Wq8", [DIM, DIM], I8)
    wsc = I("wsc", [128, 8])
    fbias = I("fbias", [128, DIM])
    nw = I("nw", [128, 16])
    cwx = I("cwx", [128, 64])
    cbx = I("cbx", [128, 16])
    cwB = I("cwB", [128, 4]); cbB = I("cbB", [128, 1])
    dtbi = I("dtb", [128, 1])
    alog = I("alog", [128, 1])
    dpc = I("dpc", [128, 32])
    gate = I("gate", [128, 1])
    IDb = I("IDb", [128, 128], BF16)
    NBI = I("NBI", [128, 128], BF16)
    SLO = I("SLO", [128, 128], BF16)
    OHT = I("OHT", [128, 4096], BF16)
    ONEB = I("ONEB", [128, 128], BF16)
    CVALS = I("CVALS", [128, len(CONSTVALS)])
    R3F = I("R3F", [32, 96])
    IDf = I("IDf", [128, 128], F32)

    OUT = nc.dram_tensor("OUT", [T, DIM], F32, kind="ExternalOutput").ap()

    xnf_scr = nc.dram_tensor("xnf_scr", [DIM, ST], F32, kind="Internal").ap()
    xbcx_scr = nc.dram_tensor("xbcx_scr", [DIN, T], F32, kind="Internal").ap()
    bc_scr = nc.dram_tensor("bc_scr", [128, T], F32, kind="Internal").ap()
    dt_scr = nc.dram_tensor("dt_scr", [32, T], F32, kind="Internal").ap()
    xt_scr = nc.dram_tensor("xt_scr", [T, DIN], BF16, kind="Internal").ap()
    wf_scr = nc.dram_tensor("wf_scr", [DIN, DIM], BF16, kind="Internal").ap()
    mx_in = nc.dram_tensor("mx_in", [128, 1], F32, kind="Internal").ap()
    mx_out = nc.dram_tensor("mx_out", [128, 1], F32, kind="Internal").ap()
    ag_in = nc.dram_tensor("ag_in", [64, DIN], F32, kind="Internal").ap()
    ag_out = nc.dram_tensor("ag_out", [128, DIN], F32, kind="Internal").ap()
    xq_scr = nc.dram_tensor("xq_scr", [128, 8 * TT], BF16, kind="Internal").ap()

    dbg_specs = {
        "dbg_xnf": ([DIM, ST], F32), "dbg_xbcx": ([DIN, T], F32),
        "dbg_bc": ([128, T], F32), "dbg_dt": ([32, T], F32),
        "dbg_ht": ([64, DIN], F32), "dbg_xt": ([T, DIN], BF16),
        "dbg_siluz": ([128, DIN], F32), "dbg_yb": ([128, DIN], BF16),
        "dbg_xqs": ([128, 8 * TT], BF16), "dbg_rcol": ([128, 2], F32),
        "dbg_xw": ([128, DIN], BF16), "dbg_wts": ([128, 34], F32),
        "dbg_ab": ([128, 32], F32), "dbg_dsm": ([128, 2 * TT + 7 * Q + 272], F32),
        "dbg_bcb": ([128, 2 * Q + 64], BF16),
        "dbg_pc": ([128, 2048], BF16),
    }
    dbg_outs = {n: nc.dram_tensor(n, s, d, kind="ExternalOutput").ap()
                for n, (s, d) in dbg_specs.items() if n in dbg}

    with ExitStack() as ctx:
        def sb(name, f, dtype=F32, p=128):
            return ctx.enter_context(nc.sbuf_tensor(name, [p, f], dtype))

        idb = sb("idb", 128, BF16)
        nbi = sb("nbi", 128, BF16)
        slo = sb("slo", 128, BF16)
        oneb = sb("oneb", 128, BF16)
        idf = sb("idf", 128)
        oht = sb("oht", 4096, BF16)
        cwxs = sb("cwxs", 64); cbxs = sb("cbxs", 16)
        cwBs = sb("cwBs", 4); cbBs = sb("cbBs", 1)
        dtbs = sb("dtbs", 1)
        negA = sb("negA", 1)
        dpcs = sb("dpcs", 32)
        dpcb = sb("dpcb", 32, BF16)
        wTb = sb("wTb", 64, BF16)
        gats = sb("gats", 1)
        nws = sb("nws", 16)
        wscs = sb("wscs", 8)
        fbb = sb("fbb", DIM, BF16)
        wintb = sb("wintb", 8 * 2208, BF16)
        wktb = sb("wktb", 8 * 1024, BF16)
        wqtb = sb("wqtb", 8 * 1024, BF16)
        wstg = sb("wstg", 2208)
        w8i = sb("w8i", 2048, I8)
        ht = sb("ht", DIN)
        htb = sb("htb", DIN, BF16)
        stg = sb("stg", 3 * DIM)
        xnb = sb("xnb", 8 * TW, BF16)
        mxc = sb("mxc", 2)
        mvst = sb("mvst", 16)
        xbcb = sb("xbcb", 2048, BF16)
        Bt = sb("Bt", TT)
        Ct = sb("Ct", TT)
        dtt = sb("dtt", TT)
        dsm = sb("dsm", 2 * TT + 7 * Q + 264 + 8)
        sHL = sb("sHL", 2 * Q, BF16)
        LHL = sb("LHL", 2 * Q, BF16)
        bcb = sb("bcb", 2 * (2 * Q + 64), BF16)
        wTs = sb("wTs", 34)
        esT = sb("esT", 64)
        aB = sb("aB", 64)
        XT = sb("XT", 2 * DIN, BF16)
        Xw = sb("Xw", DIN, BF16)
        ssq = sb("ssq", 4)
        rcol = sb("rcol", 2)
        mrow = sb("mrow", 360)
        cpool = sb("cpool", len(CONSTVALS))
        resb = sb("resb", 2048)
        r3f = sb("r3f", 96, p=32)

        dbgpc = sb("dbgpc", 2048, BF16) if "dbg_pc" in dbg_outs else None
        actx = ExitStack()
        def sbA(name, f, dtype=F32, p=128):
            return actx.enter_context(nc.sbuf_tensor(name, [p, f], dtype))
        xg = sbA("xg", DIM)
        xnt = sbA("xnt", DIM)
        xbcx = sbA("xbcx", 16 * TT)
        cvt = sbA("cvt", 4 * TT)

        sems = {}
        semnames = ["pe", "dve", "act", "pool", "sy", "cc"]
        for q in ("dS", "dV", "dA", "dG"):
            for w in range(Prog.DMA_W):
                semnames.append(f"{q}{w}")
        for s in semnames:
            sems[s] = ctx.enter_context(nc.semaphore(s))

        PA = ctx.enter_context(nc.psum_tensor("PA", [128, 1024], F32))
        PB = ctx.enter_context(nc.psum_tensor("PB", [128, 1024], F32))
        PC = ctx.enter_context(nc.psum_tensor("PC", [128, 1024], F32))
        PS = ctx.enter_context(nc.psum_tensor("PS", [128, 512], F32))
        PT = ctx.enter_context(nc.psum_tensor("PT", [128, 1024], BF16))

        P = Prog(sems)
        dbgflag = {"done": False}

        def ld(dst, src, key, eng="sync"):
            keys = key if isinstance(key, tuple) else (key,)
            P.op(eng, lambda e, d=dst, s=src: e.dma_start(out=d, in_=s),
                 writes=keys, dma=True)

        for t_, s_, k_ in [(idb, IDb, "idb"), (nbi, NBI, "nbi"), (slo, SLO, "slo"),
                           (oneb, ONEB, "oneb"), (idf, IDf, "idf"), (oht, OHT, "oht"),
                           (cwxs, cwx, "cwxs"), (cbxs, cbx, "cbxs"),
                           (cwBs, cwB, "cwBs"), (cbBs, cbB, "cbBs"),
                           (dtbs, dtbi, "dtbs"), (dpcs, dpc, "dpcs"),
                           (gats, gate, "gats"), (nws, nw, "nws"),
                           (wscs, wsc, "wscs"), (cpool, CVALS, "cpool"), (r3f, R3F, "r3f"),
                           (negA, alog, "negAr")]:
            ld(t_[:, :], s_, k_)
        for ci, cv in enumerate(CONSTVALS):
            nc.const_aps.register(F32, cv, cpool[:, ci:ci + 1])
        ld(wstg[:, 0:1024], fbias, "wstg")
        P.op("vector", lambda e: e.tensor_copy(fbb[:, :], wstg[:, 0:1024]),
             reads=("wstg",), writes=("fbb",))
        P.op("scalar", lambda e: e.activation(negA[:, :], negA[:, :], AF.Exp),
             reads=("negAr", "cpool"), writes=("negA1",))
        P.op("scalar", lambda e: e.mul(negA[:, :], negA[:, :], -1.0),
             reads=("negA1",), writes=("negA",))
        P.op("vector", lambda e: e.memset(ht[0:64, :], 0.0), writes=("ht",))
        P.op("vector", lambda e: e.memset(mxc[:, :], 0.0), writes=("mxc",))
        P.op("vector", lambda e: e.memset(mrow[0:1, 32:224], 1.0), writes=("onesr",))
        P.op("vector", lambda e: e.tensor_copy(dpcb[:, :], dpcs[:, :]),
             reads=("dpcs",), writes=("dpcb",))

        STGALL = ("stg0_0", "stg0_1", "stg1_0", "stg1_1", "stg2_0", "stg2_1")
        XB9 = tuple(f"xbcx{c}" for c in range(9))
        SLOT3 = [(lambda w: wstg[:, 0:w], ("wstg",)),
                 (lambda w: stg[:, 0:w], STGALL),
                 (lambda w: xbcx[:, 0:w], XB9)]
        SLOT5 = [(lambda w: wstg[:, 0:w], ("wstg",)),
                 (lambda w: stg[:, 0:w], ("stg0_0", "stg0_1")),
                 (lambda w: stg[:, 1024:1024 + w], ("stg1_0", "stg1_1")),
                 (lambda w: xbcx[:, 0:w], tuple(f"xbcx{c}" for c in range(4))),
                 (lambda w: xbcx[:, 1024:1024 + w], tuple(f"xbcx{c}" for c in range(4, 8)))]
        CENG = ["vector", "scalar", "vector"]
        for d in range(8):
            slf, keys = SLOT3[d % 3]
            sl = slf(2208)
            ld(sl, WinT[d * 128:(d + 1) * 128, 2048:4256], keys)
            eng = CENG[d % 3]
            if eng == "scalar":
                P.op("scalar", lambda e, d=d, sl=sl: e.copy(
                    wintb[:, d * 2208:(d + 1) * 2208], sl),
                     reads=keys, writes=(f"wintb{d}",))
            else:
                P.op(eng, lambda e, d=d, sl=sl: e.tensor_copy(
                    wintb[:, d * 2208:(d + 1) * 2208], sl),
                     reads=keys, writes=(f"wintb{d}",))
        for d in range(8):
            slf, keys = SLOT5[d % 5]
            sl = slf(1024)
            ld(sl, WkT[d * 128:(d + 1) * 128, :], keys)
            eng = CENG[d % 3]
            if eng == "scalar":
                P.op("scalar", lambda e, d=d, sl=sl: e.copy(
                    wktb[:, d * 1024:(d + 1) * 1024], sl),
                     reads=keys, writes=(f"wkb{d}",))
            else:
                P.op(eng, lambda e, d=d, sl=sl: e.tensor_copy(
                    wktb[:, d * 1024:(d + 1) * 1024], sl),
                     reads=keys, writes=(f"wkb{d}",))

        def ln_group(g):
            gs = ""
            xg_s = xg[:, :]
            xn_s = xnt[:, :]
            ld(xg_s, Xc[g * 128:(g + 1) * 128, :], "xg")
            P.op("vector", lambda e, xg_s=xg_s: e.bn_stats(mvst[:, 0:6], xg_s[:, 0:512]),
                 reads=("xg",), writes=("bn6",))
            P.op("vector", lambda e, xg_s=xg_s: e.bn_stats(mvst[:, 6:12], xg_s[:, 512:1024]),
                 reads=("xg",), writes=("bn6",))
            P.op("vector", lambda e: e.bn_aggr(mvst[:, 12:14], mvst[:, 0:12]),
                 reads=("bn6",), writes=("bn2",))
            P.op("scalar", lambda e: e.activation(mvst[:, 14:15], mvst[:, 13:14], AF.Ln, bias=EPS),
                 reads=("bn2",), writes=("lnv",))
            P.op("scalar", lambda e: e.activation(mvst[:, 14:15], mvst[:, 14:15], AF.Exp, scale=-0.5),
                 reads=("lnv",), writes=("rsv",))
            P.op("vector", lambda e, xg_s=xg_s, xn_s=xn_s: e.tensor_scalar(
                xn_s, xg_s, mvst[:, 12:13], mvst[:, 14:15], AL.subtract, AL.mult),
                 reads=("xg", "bn2", "rsv"), writes=("xnt",))
            P.op("vector", lambda e, xn_s=xn_s: e.tensor_reduce(
                mxc[:, 1:2], xn_s, AX, AL.max, apply_absolute_value=True),
                 reads=("xnt",), writes=("mxg",))
            P.op("vector", lambda e: e.tensor_max(mxc[:, 0:1], mxc[:, 0:1], mxc[:, 1:2]),
                 reads=("mxg", "mxc"), writes=("mxc",))
            ss = g % 3
            for half in range(2):
                pbank = PS if half == 0 else PC
                pkey = "PS.0" if half == 0 else "PC.0"
                fns = []
                for j in range(4):
                    d = half * 4 + j
                    fns.append(lambda e, d=d, j=j, xn_s=xn_s, pbank=pbank: e.transpose(
                        pbank[:, j * 128:(j + 1) * 128], xn_s[:, d * 128:(d + 1) * 128], idf[:, :]))
                P.mms(fns, reads=("xnt", "idf"), writes=(pkey,))
                eng = "vector" if half == 0 else "scalar"
                def ev(e, half=half, ss=ss, eng=eng, pbank=pbank):
                    dst = stg[:, ss * DIM + half * 512: ss * DIM + (half + 1) * 512]
                    return e.tensor_copy(dst, pbank[:, 0:512]) if eng == "vector" \
                        else e.copy(dst, pbank[:, 0:512])
                P.op(eng, ev, reads=(pkey,), writes=(f"stg{ss}_{half}",))
            def spill(e, g=g, ss=ss):
                src = stg[:, ss * DIM:(ss + 1) * DIM].rearrange("p (d t) -> p d t", d=8)
                dst = xnf_scr.rearrange("(d p) t -> p d t", d=8)[:, :, g * 128:(g + 1) * 128]
                return e.dma_start(out=dst, in_=src)
            P.op("scalar", spill, reads=(f"stg{ss}_0", f"stg{ss}_1"),
                 writes=(f"xnfg{g}",), dma=True)

        def xnb_build(i):
            lo = i * 256
            off = 0
            while off < TW:
                g = (lo + off) // 128
                wi = (lo + off) % 128
                tk = min(128 - wi, TW - off)
                ss = g % 3
                for d in range(8):
                    def xc(e, d=d, ss=ss, wi=wi, off=off, tk=tk):
                        dst = xnb[:, d * TW + off: d * TW + off + tk]
                        s_ = stg[:, ss * DIM + d * 128 + wi: ss * DIM + d * 128 + wi + tk]
                        return e.tensor_copy(dst, s_) if d % 2 == 0 else e.copy(dst, s_)
                    P.op("vector" if d % 2 == 0 else "scalar", xc,
                         reads=(f"stg{ss}_{0 if d < 4 else 1}",), writes=(f"xnb{d}",))
                off += tk

        XNBKEYS = tuple(f"xnb{d}" for d in range(8))
        WINKEYS = tuple(f"wintb{d}" for d in range(8))

        def conv_tap(kind, c, k, psrc, pkeys):
            sl = c % 4
            wv = cwxs[:, c * 4:c * 4 + 4] if kind == "x" else cwBs[:, :]
            acc = cvt[:, sl * TT:(sl + 1) * TT]
            if k == 0:
                P.op("scalar", lambda e, acc=acc, psrc=psrc, wv=wv: e.activation(
                    acc, psrc[:, 0:TT], AF.Copy, scale=wv[:, 0:1]),
                     reads=pkeys, writes=(f"cacc{sl}",))
            else:
                P.op("vector", lambda e, acc=acc, psrc=psrc, wv=wv, k=k: e.scalar_tensor_tensor(
                    acc, psrc[:, k:k + TT], wv[:, k:k + 1], acc, AL.mult, AL.add),
                     reads=pkeys + (f"cacc{sl}",), writes=(f"cacc{sl}",))

        def conv_silu(kind, c):
            sl = c % 4
            acc = cvt[:, sl * TT:(sl + 1) * TT]
            if kind == "x":
                bv = cbxs[:, c:c + 1]
                dst, dkey = xbcx[:, c * TT:(c + 1) * TT], f"xbcx{c}"
            else:
                bv = cbBs[:, 0:1]
                dst, dkey = Bt[:, :], "Bt"
            P.op("scalar", lambda e, dst=dst, acc=acc, bv=bv: e.activation(
                dst, acc, AF.Silu, bias=bv),
                 reads=(f"cacc{sl}",), writes=(dkey,))

        def inproj_A():
            banks = [(PA, 0, "PA.0"), (PA, 512, "PA.1"), (PB, 0, "PB.0"), (PB, 512, "PB.1")]
            def pair_srcs(p):
                out = []
                for c in (2 * p, 2 * p + 1):
                    pb, off, pk = banks[c % 4]
                    out.append((c, pb[:, off:off + TW], (pk,)))
                return out

            prev = None
            for p in range(8):
                if p >= 2:
                    conv_silu("x", 2 * p - 4)
                    conv_silu("x", 2 * p - 3)
                for c in (2 * p, 2 * p + 1):
                    pb, off, pk = banks[c % 4]
                    fns = []
                    for d in range(8):
                        fns.append(lambda e, c=c, d=d, pb=pb, off=off: e.matmul(
                            pb[:, off:off + TW],
                            wintb[:, d * 2208 + c * 128: d * 2208 + (c + 1) * 128],
                            xnb[:, d * TW:(d + 1) * TW], start=(d == 0), stop=(d == 7)))
                    P.mms(fns, reads=WINKEYS + XNBKEYS, writes=(pk,))
                    conv_tap("x", c, 0, pb[:, off:off + TW], (pk,))
                if prev is not None:
                    for k in (1, 2, 3):
                        for c, psrc, pks_ in prev:
                            conv_tap("x", c, k, psrc, pks_)
                prev = pair_srcs(p)
            conv_silu("x", 12)
            conv_silu("x", 13)
            fns = []
            for d in range(8):
                fns.append(lambda e, d=d: e.matmul(
                    PC[:, 0:TW],
                    wintb[:, d * 2208 + 2048: d * 2208 + 2176],
                    xnb[:, d * TW:(d + 1) * TW], start=(d == 0), stop=(d == 7)))
            P.mms(fns, reads=WINKEYS + XNBKEYS, writes=("PC.0",))
            conv_tap("BC", 17, 0, PC[:, 0:TW], ("PC.0",))
            for k in (1, 2, 3):
                for c, psrc, pks_ in prev:
                    conv_tap("x", c, k, psrc, pks_)
            conv_silu("x", 14)
            conv_silu("x", 15)
            for k in (1, 2, 3):
                conv_tap("BC", 17, k, PC[:, 0:TW], ("PC.0",))
            conv_silu("BC", 17)
            fns = []
            for d in range(8):
                fns.append(lambda e, d=d: e.matmul(
                    PS[0:32, 0:TW],
                    wintb[:, d * 2208 + 2176: d * 2208 + 2208],
                    xnb[:, d * TW:(d + 1) * TW], start=(d == 0), stop=(d == 7)))
            P.mms(fns, reads=WINKEYS + XNBKEYS, writes=("PS.0",))
            u = dsm[0:32, 0:TT]; au = dsm[0:32, TT:2 * TT]
            P.op("vector", lambda e: e.tensor_scalar(u, PS[0:32, 3:TW], dtbs[0:32, :],
                                                     None, AL.add),
                 reads=("PS.0", "dtbs"), writes=("spu",))
            P.op("scalar", lambda e: e.activation(au, u, AF.Abs),
                 reads=("spu",), writes=("spau",))
            P.op("scalar", lambda e: e.activation(au, au, AF.Exp, scale=-1.0),
                 reads=("spau",), writes=("spe",))
            P.op("scalar", lambda e: e.activation(au, au, AF.Ln, bias=1.0),
                 reads=("spe",), writes=("spl",))
            ur = dsm[0:32, 2 * TT + 7 * Q + 8:2 * TT + 7 * Q + 8 + TT]
            P.op("vector", lambda e, ur=ur: e.tensor_scalar_max(ur, u, 0.0),
                 reads=("spu",), writes=("spr",))
            P.op("vector", lambda e, ur=ur: e.tensor_add(dtt[0:32, :], au, ur),
                 reads=("spl", "spr"), writes=("dtt",))

        XBCXKEYS = tuple(f"xbcx{c}" for c in range(16))

        def xt_build():
            for gg in range(2):
                src = xbcx[:, :].rearrange("p (c t) -> p c t", c=16)[:, :, gg * 128:(gg + 1) * 128]
                dst = xbcb[:, 0:2048].rearrange("p (c t) -> p c t", c=16)
                P.op("scalar", lambda e, src=src, dst=dst: e.copy(dst, src),
                     reads=XBCXKEYS, writes=("xbcb",))
                for hb in range(2):
                    fns = []
                    for j in range(8):
                        c = hb * 8 + j
                        fns.append(lambda e, c=c, j=j: e.transpose(
                            PT[:, j * 128:(j + 1) * 128], xbcb[:, c * 128:(c + 1) * 128], idb[:, :]))
                    P.mms(fns, reads=("xbcb", "idb"), writes=("PT.0",))
                    P.op("vector", lambda e, gg=gg, hb=hb: e.tensor_copy(
                        XT[:, gg * DIN + hb * 1024: gg * DIN + (hb + 1) * 1024], PT[:, :]),
                         reads=("PT.0",), writes=(f"XT{gg}",))

        def chunk_small(qq, phase):
            c0 = qq * Q
            o = 2 * TT
            a = dsm[0:32, o:o + Q]
            s = dsm[0:32, o + Q:o + 2 * Q]
            L = dsm[0:32, o + 2 * Q:o + 3 * Q]
            lnd = dsm[0:32, o + 3 * Q:o + 4 * Q]
            wv = dsm[0:32, o + 4 * Q:o + 5 * Q]
            alc = dsm[0:32, o + 5 * Q:o + 5 * Q + 1]
            P.op("vector", lambda e, a=a, c0=c0: e.tensor_scalar(
                a, dtt[0:32, c0:c0 + Q], negA[0:32, :], None, AL.mult),
                 reads=("dtt", "negA"), writes=("csa",))
            P.op("vector", lambda e, s=s, a=a: e.tensor_tensor_scan(
                s, cpool[0:32, 1:2].broadcast_to([32, Q]), a, 0.0, AL.mult, AL.add),
                 reads=("csa", "cpool"), writes=("css",))
            P.op("vector", lambda e, lnd=lnd, c0=c0: e.tensor_scalar_max(
                lnd, dtt[0:32, c0:c0 + Q], 1e-30),
                 reads=("dtt",), writes=("csd",))
            P.op("scalar", lambda e, lnd=lnd: e.activation(lnd, lnd, AF.Ln),
                 reads=("csd",), writes=("cslnd",))
            P.op("vector", lambda e, L=L, lnd=lnd, s=s: e.tensor_sub(L, lnd, s),
                 reads=("cslnd", "css"), writes=("csL",))
            P.op("vector", lambda e, wv=wv, L=L, s=s: e.tensor_scalar(
                wv, L, s[:, Q - 1:Q], None, AL.add),
                 reads=("csL", "css"), writes=("cswa",))
            P.op("scalar", lambda e, wv=wv: e.activation(wv, wv, AF.Exp),
                 reads=("cswa",), writes=("csw",))
            P.mms([lambda e, wv=wv: e.transpose(PS[0:128, 128:160], wv, idf[0:32, 0:32])],
                  reads=("csw", "idf"), writes=("PS.0",))
            P.op("vector", lambda e, qq=qq: e.tensor_copy(
                wTb[:, qq * 32:qq * 32 + 32], PS[:, 128:160]),
                 reads=("PS.0",), writes=(f"wTb{qq}",))
            P.op("scalar", lambda e, alc=alc, s=s: e.activation(alc, s[:, Q - 1:Q], AF.Exp),
                 reads=("css",), writes=("csal",))
            P.mms([lambda e, alc=alc: e.transpose(PS[0:1, 160:192], alc, idf[0:32, 0:32])],
                  reads=("csal", "idf"), writes=("PS.0",))
            P.op("vector", lambda e: e.tensor_copy(mrow[0:1, 0:32], PS[0:1, 160:192]),
                 reads=("PS.0",), writes=("alrow",))
            P.mms([lambda e: e.matmul(PS[0:64, 192:224], mrow[0:1, 32:96],
                                      mrow[0:1, 0:32], start=True, stop=True)],
                  reads=("onesr", "alrow"), writes=("PS.0",))
            P.op("vector", lambda e, qq=qq: e.tensor_copy(
                aB[0:64, qq * 32:qq * 32 + 32], PS[0:64, 192:224]),
                 reads=("PS.0",), writes=(f"aB{qq}",))
            if phase == "C":
                P.mms([lambda e, s=s: e.matmul(PS[0:96, 128:384], r3f[:, :],
                                               dsm[0:32, o + Q:o + 3 * Q],
                                               start=True, stop=True)],
                      reads=("r3f", "css", "csL"), writes=("PS.0",))
                t96 = dsm[0:96, o + 5 * Q + 8:o + 7 * Q + 8]
                for li, (dstT0, kb) in enumerate(((sHL, f"sHL{qq}"), (LHL, f"LHL{qq}"))):
                    dstT = dstT0[:, qq * Q:(qq + 1) * Q]
                    ps = PS[0:96, 128 + li * Q:128 + (li + 1) * Q]
                    tl = t96[:, li * Q:(li + 1) * Q]
                    P.op("vector", lambda e, dstT=dstT, ps=ps: e.tensor_copy(dstT[0:96, :], ps),
                         reads=("PS.0",), writes=(kb + "_0",))
                    P.op("vector", lambda e, dstT=dstT, ps=ps, tl=tl: e.tensor_tensor(
                        tl, ps, dstT[0:96, :], AL.subtract),
                         reads=("PS.0", kb + "_0"), writes=(kb + "_t",))
                    P.op("vector", lambda e, dstT=dstT, tl=tl: e.tensor_copy(
                        dstT[32:64, :], tl[32:64, :]),
                         reads=(kb + "_t",), writes=(kb + "_1",))
                    P.op("vector", lambda e, dstT=dstT, tl=tl: e.tensor_copy(
                        dstT[64:96, :], tl[64:96, :]),
                         reads=(kb + "_t",), writes=(kb + "_1",))
                    P.op("vector", lambda e, dstT=dstT, tl=tl: e.tensor_tensor(
                        tl[64:96, :], tl[64:96, :], dstT[64:96, :], AL.subtract),
                         reads=(kb + "_t", kb + "_1"), writes=(kb + "_t2",))
                    P.op("vector", lambda e, dstT=dstT, tl=tl: e.tensor_copy(
                        dstT[64:96, :], tl[64:96, :]),
                         reads=(kb + "_t2",), writes=(kb + "_2",))
                P.mms([lambda e, s=s: e.transpose(PS[0:128, 384:416], s, idf[0:32, 0:32])],
                      reads=("css", "idf"), writes=("PS.0",))
                P.op("scalar", lambda e, qq=qq: e.activation(
                    esT[:, qq * 32:(qq + 1) * 32], PS[:, 384:416], AF.Exp),
                     reads=("PS.0",), writes=(f"esT{qq}",))

        def bc_cast(qq, phase):
            c0 = qq * Q
            bo = qq * 320
            P.op("vector", lambda e, c0=c0, bo=bo: e.tensor_copy(
                bcb[0:64, bo:bo + Q], Bt[0:64, c0:c0 + Q]),
                 reads=("Bt",), writes=(f"Bb{qq}",))
            P.mms([lambda e, bo=bo: e.transpose(PT[0:128, 0:64], bcb[0:64, bo:bo + Q],
                                                idb[0:64, 0:64])],
                  reads=(f"Bb{qq}", "idb"), writes=("PT.0",))
            P.op("vector", lambda e, bo=bo: e.tensor_copy(
                bcb[:, bo + 2 * Q:bo + 2 * Q + 64], PT[:, 0:64]),
                 reads=("PT.0",), writes=(f"BTb{qq}",))
            if phase == "C":
                P.op("vector", lambda e, c0=c0, bo=bo: e.tensor_copy(
                    bcb[0:64, bo + Q:bo + 2 * Q], Ct[0:64, c0:c0 + Q]),
                     reads=("Ct",), writes=(f"Cb{qq}",))

        def xw_and_contrib(qq, phase):
            XTq = XT[:, qq * DIN:(qq + 1) * DIN]
            xtk = f"XT{qq}"
            P.op("gpsimd", lambda e, qq=qq, XTq=XTq: e.tensor_tensor(
                Xw[:, :], XTq, rep_h(wTb[:, qq * 32:qq * 32 + 32], 32, 64), AL.mult),
                 reads=(xtk, f"wTb{qq}"), writes=("Xw",))
            P.op("gpsimd", lambda e, qq=qq: e.tensor_tensor(
                ht[0:64, :], ht[0:64, :],
                rep_h(aB[0:64, qq * 32:qq * 32 + 32], 32, 64), AL.mult),
                 reads=("ht", f"aB{qq}"), writes=("ht",))
            for half in range(2):
                fns = []
                for j in range(2):
                    fns.append(lambda e, qq=qq, half=half, j=j: e.matmul(
                        PC[0:64, j * 512:(j + 1) * 512],
                        bcb[:, qq * 320 + 2 * Q: qq * 320 + 2 * Q + 64],
                        Xw[:, half * 1024 + j * 512: half * 1024 + (j + 1) * 512],
                        start=True, stop=True))
                P.mms(fns, reads=(f"BTb{qq}", "Xw"), writes=("PC.0", "PC.1"))
                P.op("vector", lambda e, half=half: e.tensor_add(
                    ht[0:64, half * 1024:(half + 1) * 1024],
                    ht[0:64, half * 1024:(half + 1) * 1024], PC[0:64, 0:1024]),
                     reads=("ht", "PC.0", "PC.1"), writes=("ht",))

        ln_group(0)
        for i in range(a_tiles):
            ln_group(2 * i + 1)
            ln_group(2 * i + 2)
            xnb_build(i)
            inproj_A()
            xt_build()
            for qq in range(2):
                chunk_small(qq, "A")
                bc_cast(qq, "A")
                xw_and_contrib(qq, "A")
            if "dbg_xbcx" in dbg_outs:
                P.op("scalar", lambda e, i=i: e.dma_start(
                    out=xbcx_scr.rearrange("(c p) t -> p c t", c=16)[:, :, i * TT:(i + 1) * TT],
                    in_=xbcx[:, :].rearrange("p (c t) -> p c t", c=16)),
                     reads=XBCXKEYS, writes=("xbspl",), dma=True)
            P.op("scalar", lambda e, i=i: e.dma_start(
                out=bc_scr[:, i * TT:(i + 1) * TT], in_=Bt[:, :]),
                 reads=("Bt",), writes=("bspl",), dma=True)
            P.op("scalar", lambda e, i=i: e.dma_start(
                out=dt_scr[:, i * TT:(i + 1) * TT], in_=dtt[0:32, :]),
                 reads=("dtt",), writes=("dtspl",), dma=True)
            P.op("scalar", lambda e, i=i: e.dma_start(
                out=xt_scr[i * TT:(i + 1) * TT, :].rearrange("(g p) c -> p g c", g=2),
                in_=XT[:, :].rearrange("p (g c) -> p g c", g=2)),
                 reads=("XT0", "XT1"), writes=("xtspl",), dma=True)

        P.op("gpsimd", lambda e: e.dma_start(out=mx_in, in_=mxc[:, 0:1]),
             reads=("mxc",), writes=("mx_in",), dma=True)
        P.op("gpsimd", lambda e: e.collective_compute(
            "AllReduce", AL.max, replica_groups=[list(range(8))],
            ins=[mx_in], outs=[mx_out]),
             reads=("mx_in",), writes=("mx_out",), sem="cc")
        P.op("gpsimd", lambda e: e.dma_start(out=ag_in, in_=ht[0:64, :]),
             reads=("ht",), writes=("ag_in",), dma=True)
        P.op("gpsimd", lambda e: e.collective_compute(
            "AllGather", AL.bypass, replica_groups=[[0, 1], [2, 3], [4, 5], [6, 7]],
            ins=[ag_in], outs=[ag_out]),
             reads=("ag_in",), writes=("ag_out",), sem="cc")
        P.op("gpsimd", lambda e: e.dma_start(out=mxc[:, 1:2], in_=mx_out),
             reads=("mx_out",), writes=("mxcc",), dma=True)
        WSLT = [(xg[:, :], ("xg",)), (xnt[:, :], ("xnt",))]
        CSLT = [(xnb[:, 0:1024], ("xnb0", "xnb1", "xnb2", "xnb3")),
                (XT[:, 1024:2048], ("XT0",))]
        MSLT = [(xnb[:, 1024:2048], ("xnb4", "xnb5", "xnb6", "xnb7")),
                (XT[:, 0:1024], ("XT0",))]
        ESLT = [(htb[:, 0:1024], ("htb",)), (htb[:, 1024:2048], ("htb",))]

        ZSLOT = [(wstg[:, 0:2048], ("wstg",)),
                 (resb[:, 0:2048], ("xout", "osb"))]

        def zreload(d):
            sl, keys = ZSLOT[d % 2]
            ld(sl, WinT[d * 128:(d + 1) * 128, 0:2048], keys)
            if d % 2 == 0:
                P.op("vector", lambda e, d=d, sl=sl: e.tensor_copy(
                    wintb[:, d * 2208:d * 2208 + 2048], sl),
                     reads=keys, writes=(f"wintb{d}",))
            else:
                P.op("scalar", lambda e, d=d, sl=sl: e.copy(
                    wintb[:, d * 2208:d * 2208 + 2048], sl),
                     reads=keys, writes=(f"wintb{d}",))

        def rcol_chain():
            P.mms([lambda e: e.transpose(PS[0:1, 0:128], mxc[:, 1:2], idf[:, :])],
                  reads=("mxcc", "idf"), writes=("PS.0",))
            P.op("vector", lambda e: e.tensor_copy(mrow[0:1, 224:352], PS[0:1, 0:128]),
                 reads=("PS.0",), writes=("mrow",))
            P.op("vector", lambda e: e.tensor_reduce(
                mrow[0:1, 352:353], mrow[0:1, 224:352], AX, AL.max),
                 reads=("mrow",), writes=("Mg",))
            P.op("vector", lambda e: e.tensor_scalar(
                mrow[0:1, 353:354], mrow[0:1, 352:353], 1.0 / 127.0, 1e-5, AL.mult, AL.max),
                 reads=("Mg",), writes=("xsc",))
            P.op("vector", lambda e: e.reciprocal(mrow[0:1, 354:355], mrow[0:1, 353:354]),
                 reads=("xsc",), writes=("xsr",))
            P.mms([lambda e: e.matmul(PS[:, 128:129], mrow[0:1, 96:224],
                                      mrow[0:1, 354:355], start=True, stop=True),
                   lambda e: e.matmul(PS[:, 129:130], mrow[0:1, 96:224],
                                      mrow[0:1, 353:354], start=True, stop=True)],
                  reads=("onesr", "xsr", "xsc"), writes=("PS.0",))
            P.op("vector", lambda e: e.tensor_copy(rcol[:, 0:2], PS[:, 128:130]),
                 reads=("PS.0",), writes=("rcol",))

        def hgate():
            STGKEYS = ("stg0_0", "stg0_1", "stg1_0", "stg1_1", "stg2_0", "stg2_1")
            P.op("gpsimd", lambda e: e.dma_start(out=stg[0:64, 0:DIN], in_=ag_out[0:64, :]),
                 reads=("ag_out",) + STGKEYS, writes=("hbs",), dma=True)
            P.op("vector", lambda e: e.tensor_scalar(
                ht[0:64, :], stg[0:64, 0:DIN], gats[0:64, :], None, AL.mult),
                 reads=("hbs", "gats", "ht"), writes=("ht",))

        def wqs(n):
            ws = n % 2
            ld(w8i[:, ws * 1024:(ws + 1) * 1024], Wq8[n * 128:(n + 1) * 128, :], f"w8i{ws}")
            P.op("vector", lambda e, ws=ws: e.tensor_copy(
                pool[:, 1024:2048], w8i[:, ws * 1024:(ws + 1) * 1024]),
                 reads=(f"w8i{ws}",), writes=("slz1",))
            P.op("vector", lambda e, n=n: e.tensor_scalar(
                xbcb[:, 0:1024], pool[:, 1024:2048], wscs[:, n:n + 1], rcol[:, 1:2],
                AL.mult, AL.mult),
                 reads=("slz1", "wscs", "rcol"), writes=("xbcb",))
            fns = []
            for j in range(8):
                fns.append(lambda e, j=j: e.transpose(
                    PT[:, j * 128:(j + 1) * 128], xbcb[:, j * 128:(j + 1) * 128], idb[:, :]))
            P.mms(fns, reads=("xbcb", "idb"), writes=("PT.0",))
            def evq(e, n=n):
                dst = wqtb[:, :].rearrange("p (k n2) -> p k n2", k=8)[:, :, n * 128:(n + 1) * 128]
                return e.tensor_copy(dst, PT[:, :].rearrange("p (k n2) -> p k n2", k=8))
            P.op("vector", evq, reads=("PT.0",), writes=("wqtb",))

        for db in range(16):
            s2 = db % 2
            wsl, wk_ = WSLT[s2]
            csl, ck_ = CSLT[s2]
            msl, mk_ = MSLT[s2]
            esl, ek_ = ESLT[s2]
            ld(wsl, WoT[db * 128:(db + 1) * 128, :], wk_)
            P.op("vector", lambda e, csl=csl, wsl=wsl: e.tensor_copy(csl, wsl),
                 reads=wk_, writes=ck_)
            fns = []
            for j in range(8):
                fns.append(lambda e, j=j, csl=csl: e.transpose(
                    PT[:, j * 128:(j + 1) * 128], csl[:, j * 128:(j + 1) * 128], idb[:, :]))
            P.mms(fns, reads=ck_ + ("idb",), writes=("PT.0",))
            P.op("vector", lambda e, msl=msl: e.tensor_copy(msl, PT[:, :]),
                 reads=("PT.0",), writes=mk_)
            pb = PA if s2 == 0 else PB
            pks = ("PA.0", "PA.1") if s2 == 0 else ("PB.0", "PB.1")
            fns = []
            for m in range(8):
                for j in range(2):
                    fns.append(lambda e, m=m, j=j, msl=msl, pb=pb: e.matmul(
                        pb[:, j * 512:(j + 1) * 512], msl[:, m * 128:(m + 1) * 128],
                        wktb[:, m * 1024 + j * 512: m * 1024 + (j + 1) * 512],
                        start=(m == 0), stop=(m == 7)))
            P.mms(fns, reads=mk_ + tuple(f"wkb{m}" for m in range(8)), writes=pks)
            P.op("vector", lambda e, db=db, esl=esl, pb=pb: e.tensor_scalar(
                esl, pb[:, :], nws[:, db:db + 1], None, AL.mult),
                 reads=pks + ("nws",), writes=ek_)
            P.op("scalar", lambda e, db=db, esl=esl: e.dma_start(
                out=wf_scr[db * 128:(db + 1) * 128, :], in_=esl),
                 reads=ek_, writes=("wf_scr",), dma=True)
            if db % 2 == 0:
                zreload(db // 2)
            if db == 5:
                rcol_chain()
            elif db == 7:
                hgate()
            elif db >= 8:
                wqs(db - 8)

        actx.close()
        xqs = sb("xqs", 8 * TT, BF16)
        pool = sb("pool", 2560)
        gwv = sb("gwv", 2 * 8 * 192, BF16)
        ybs = sb("ybs", DIN, BF16)
        ybT = sb("ybT", 2 * 2048, BF16)
        hbf = sb("hbf", 1024, BF16)
        dxs = sb("dxs", DIN, BF16)
        p2e = sb("p2e", 1024, BF16)
        zws = sb("zws", 1024)
        gpm = sb("gpm", 256)
        ABRIDGE = ("xg", "xnt", "cvt0", "cacc", "bn6", "bn2", "lnv", "rsv",
                   "mxg") + XBCXKEYS

        SZK1 = ("slz1",)

        def z_half(i, qq, half):
            szap = wstg[:, 0:2048] if qq == 0 else pool[:, 512:2560]
            szk = ("wstg",) if qq == 0 else SZK1
            fns = []
            for d in range(8):
                for j in range(2):
                    fns.append(lambda e, qq=qq, half=half, d=d, j=j: e.matmul(
                        PC[:, j * 512:(j + 1) * 512],
                        xnb[:, d * TW + 3 + qq * 128: d * TW + 3 + (qq + 1) * 128],
                        wintb[:, d * 2208 + half * 1024 + j * 512:
                              d * 2208 + half * 1024 + (j + 1) * 512],
                        start=(d == 0), stop=(d == 7)))
            P.mms(fns, reads=XNBKEYS + WINKEYS, writes=("PC.0", "PC.1"))
            P.op("scalar", lambda e: e.activation(
                zws[:, :], PC[:, :], AF.Sigmoid),
                 reads=("PC.0", "PC.1"), writes=("zws",))
            P.op("vector", lambda e, half=half, szap=szap: e.tensor_tensor(
                szap[:, half * 1024:(half + 1) * 1024],
                PC[:, :], zws[:, :], AL.mult),
                 reads=("PC.0", "PC.1", "zws"), writes=szk)

        def z_proj(i, qq):
            z_half(i, qq, 0)
            z_half(i, qq, 1)

        ybanks = ("PA.0", "PA.1", "PB.0", "PB.1")

        def scans_c(i, qq):
            chunk_small(qq, "C")
            bc_cast(qq, "C")
            bo = qq * 320
            P.mms([lambda e, bo=bo: e.matmul(PS[:, 256:384], bcb[0:64, bo:bo + Q],
                                             bcb[0:64, bo + Q:bo + 2 * Q],
                                             start=True, stop=True)],
                  reads=(f"Bb{qq}", f"Cb{qq}"), writes=("PS.0",))
            P.op("scalar", lambda e, qq=qq: e.copy(gpm[:, qq * 128:(qq + 1) * 128],
                                                   PS[:, 256:384]),
                 reads=("PS.0",), writes=(f"gpm{qq}",))

        def wbuild(i, qq):
            XTq = XT[:, qq * DIN:(qq + 1) * DIN]
            xtk = f"XT{qq}"
            SLK = (f"sHL{qq}_0", f"sHL{qq}_1", f"sHL{qq}_2",
                   f"LHL{qq}_0", f"LHL{qq}_1", f"LHL{qq}_2")
            sQ = sHL[0:96, qq * Q:(qq + 1) * Q]
            lQ = LHL[0:96, qq * Q:(qq + 1) * Q]
            def intra_g(g):
                gs = g % 2
                pb = PA if g < 2 else PB
                bank = ybanks[g]
                fns = []
                for j in range(8):
                    h = g * 8 + j
                    off = (h % 16) * 64
                    fns.append(lambda e, h=h, j=j, gs=gs, pb=pb, off=off, XTq=XTq: e.matmul(
                        pb[:, off:off + 64],
                        gwv[:, gs * 1536 + j * 192: gs * 1536 + j * 192 + 128],
                        XTq[:, h * 64:(h + 1) * 64], start=(j == 0), stop=False))
                P.mms(fns, reads=(f"gwvA{gs}0", f"gwvA{gs}1", xtk), writes=(bank,))

            for g in range(4):
                gs = g % 2
                for hb in range(2):
                    pck = "PC.0" if hb == 0 else "PC.1"
                    fns = []
                    for j in range(4):
                        h = g * 8 + hb * 4 + j
                        fns.append(lambda e, h=h, hb=hb, j=j, sQ=sQ: e.matmul(
                            PC[:, (hb * 4 + j) * 128:(hb * 4 + j + 1) * 128],
                            oht[0:96, h * 128:(h + 1) * 128], sQ,
                            start=(j == 0), stop=False))
                    fns.append(lambda e, g=g, hb=hb, lQ=lQ: e.matmul(
                        PC[:, hb * 512:(hb + 1) * 512], lQ,
                        oht[0:96, (g * 8 + hb * 4) * 128:(g * 8 + hb * 4 + 4) * 128],
                        start=False, stop=False))
                    fns.append(lambda e, hb=hb: e.matmul(
                        PC[:, hb * 512:(hb + 1) * 512], nbi[:, :],
                        rep_t(slo[:, :], 4), start=False, stop=True))
                    P.mms(fns, reads=SLK + ("oht", "nbi", "slo"), writes=(pck,))
                    hk = "hbfA" if hb == 0 else "hbfB"
                    P.op("scalar", lambda e, hb=hb: e.activation(
                        hbf[:, hb * 512:(hb + 1) * 512],
                        PC[:, hb * 512:(hb + 1) * 512], AF.Exp),
                         reads=(pck,), writes=(hk,))
                    def gwm(e, qq=qq, gs=gs, hb=hb):
                        dst = gwv[:, gs * 1536 + hb * 768: gs * 1536 + hb * 768 + 768] \
                            .rearrange("p (h c) -> p h c", h=4)[:, :, 0:128]
                        return e.tensor_tensor(
                            dst,
                            hbf[:, hb * 512:(hb + 1) * 512].rearrange("p (h c) -> p h c", h=4),
                            rep_t(gpm[:, qq * 128:(qq + 1) * 128], 4), AL.mult)
                    P.op("vector", gwm, reads=(hk, f"gpm{qq}"), writes=(f"gwvA{gs}{hb}",))
                if g >= 1:
                    intra_g(g - 1)
            intra_g(3)
            P.op("gpsimd", lambda e, XTq=XTq: e.tensor_tensor(
                dxs[:, :], XTq, rep_h(dpcb[:, :], 32, 64), AL.mult),
                 reads=(xtk, "dpcb"), writes=("dxs",))
            fns = []
            for j in range(4):
                pb = PA if j < 2 else PB
                fns.append(lambda e, j=j, pb=pb: e.matmul(
                    pb[:, (j % 2) * 512:(j % 2 + 1) * 512], idb[:, :],
                    dxs[:, j * 512:(j + 1) * 512], start=False, stop=False))
            P.mms(fns, reads=("dxs", "idb"), writes=ybanks)
            P.op("scalar", lambda e: e.copy(htb[0:64, :], ht[0:64, :]),
                 reads=("ht",), writes=("htb",))
            P.op("vector", lambda e, qq=qq: e.tensor_tensor(
                ht[0:64, :], ht[0:64, :],
                rep_h(aB[0:64, qq * 32:qq * 32 + 32], 32, 64), AL.mult),
                 reads=("ht", f"aB{qq}"), writes=("ht",))
            for half in range(2):
                fns = []
                for j in range(2):
                    fns.append(lambda e, qq=qq, half=half, j=j: e.matmul(
                        PC[:, j * 512:(j + 1) * 512],
                        bcb[0:64, qq * 320 + Q: qq * 320 + 2 * Q],
                        htb[0:64, half * 1024 + j * 512: half * 1024 + (j + 1) * 512],
                        start=True, stop=True))
                P.mms(fns, reads=(f"Cb{qq}", "htb"), writes=("PC.0", "PC.1"))
                def p2ev(e, qq=qq, half=half):
                    src1 = esT[:, qq * 32 + half * 16: qq * 32 + (half + 1) * 16] \
                        .unsqueeze(2).broadcast_to([128, 16, 64])
                    return e.tensor_tensor(p2e[:, :], PC[:, :], src1, AL.mult)
                P.op("vector", p2ev, reads=("PC.0", "PC.1", f"esT{qq}"), writes=("p2e",))
                fns = []
                for j in range(2):
                    pb = PA if half == 0 else PB
                    fns.append(lambda e, half=half, j=j, pb=pb: e.matmul(
                        pb[:, j * 512:(j + 1) * 512], idb[:, :],
                        p2e[:, j * 512:(j + 1) * 512], start=False, stop=True))
                P.mms(fns, reads=("p2e", "idb"), writes=ybanks)
            xw_and_contrib_C(qq)

        def glue_c(i, qq):
            szap = wstg[:, 0:2048] if qq == 0 else pool[:, 512:2560]
            szk = ("wstg",) if qq == 0 else SZK1
            for half in range(2):
                pb = PA if half == 0 else PB
                P.op("vector", lambda e, half=half, pb=pb, szap=szap: e.tensor_tensor(
                    ybs[:, half * 1024:(half + 1) * 1024],
                    pb[:, :], szap[:, half * 1024:(half + 1) * 1024], AL.mult),
                     reads=ybanks + szk, writes=("ybs",))
            for b4 in range(4):
                P.op("vector", lambda e, b4=b4: e.bn_stats(
                    dsm[0:128, 1524 + b4 * 6:1530 + b4 * 6],
                    ybs[:, b4 * 512:(b4 + 1) * 512]),
                     reads=("ybs",), writes=("gbn6",))
            P.op("vector", lambda e: e.bn_aggr(dsm[0:128, 1550:1552],
                                               dsm[0:128, 1524:1548]),
                 reads=("gbn6",), writes=("gbn2",))
            P.op("vector", lambda e: e.tensor_tensor(
                ssq[:, 0:1], dsm[0:128, 1550:1551], dsm[0:128, 1550:1551], AL.mult),
                 reads=("gbn2",), writes=("gms",))
            P.op("vector", lambda e: e.tensor_add(
                ssq[:, 1:2], ssq[:, 0:1], dsm[0:128, 1551:1552]),
                 reads=("gms", "gbn2"), writes=("ssqc",))
            P.op("scalar", lambda e: e.activation(ssq[:, 1:2], ssq[:, 1:2], AF.Ln,
                                                  bias=EPS),
                 reads=("ssqc",), writes=("lnss",))
            P.op("scalar", lambda e: e.activation(ssq[:, 1:2], ssq[:, 1:2], AF.Exp, scale=-0.5),
                 reads=("lnss",), writes=("rs",))
            P.op("vector", lambda e: e.tensor_scalar(
                ybs[:, 0:1024], ybs[:, 0:1024], ssq[:, 1:2], None, AL.mult),
                 reads=("ybs", "rs"), writes=("ybs",))
            P.op("scalar", lambda e: e.activation(
                ybs[:, 1024:2048], ybs[:, 1024:2048], AF.Copy, scale=ssq[:, 1:2]),
                 reads=("ybs", "rs"), writes=("ybs",))

        def ybt_c(qq):
            for hb in range(2):
                fns = []
                for j in range(8):
                    c = hb * 8 + j
                    fns.append(lambda e, c=c, j=j: e.transpose(
                        PT[:, j * 128:(j + 1) * 128],
                        ybs[:, c * 128:(c + 1) * 128], idb[:, :]))
                P.mms(fns, reads=("ybs", "idb"), writes=("PT.0",))
                def ybev(e, qq=qq, hb=hb):
                    dst = ybT[:, qq * 2048 + hb * 1024: qq * 2048 + (hb + 1) * 1024]
                    return e.tensor_copy(dst, PT[:, :]) if hb == 0 else e.copy(dst, PT[:, :])
                P.op("vector" if hb == 0 else "scalar", ybev,
                     reads=("PT.0",), writes=(f"ybT{qq}",))

        def xw_and_contrib_C(qq):
            XTq = XT[:, qq * DIN:(qq + 1) * DIN]
            xtk = f"XT{qq}"
            P.op("gpsimd", lambda e, qq=qq, XTq=XTq: e.tensor_tensor(
                Xw[:, :], XTq, rep_h(wTb[:, qq * 32:qq * 32 + 32], 32, 64), AL.mult),
                 reads=(xtk, f"wTb{qq}"), writes=("Xw",))
            for half in range(2):
                fns = []
                for j in range(2):
                    fns.append(lambda e, qq=qq, half=half, j=j: e.matmul(
                        PC[0:64, j * 512:(j + 1) * 512],
                        bcb[:, qq * 320 + 2 * Q: qq * 320 + 2 * Q + 64],
                        Xw[:, half * 1024 + j * 512: half * 1024 + (j + 1) * 512],
                        start=True, stop=True))
                P.mms(fns, reads=(f"BTb{qq}", "Xw"), writes=("PC.0", "PC.1"))
                P.op("vector", lambda e, half=half: e.tensor_add(
                    ht[0:64, half * 1024:(half + 1) * 1024],
                    ht[0:64, half * 1024:(half + 1) * 1024], PC[0:64, 0:1024]),
                     reads=("ht", "PC.0", "PC.1"), writes=("ht",))

        def head_c(i):
            def relo(e, i=i):
                src = xnf_scr.rearrange("(d p) t -> p d t", d=8)[:, :, i * 256:i * 256 + TW]
                dst = stg[:, 0:8 * TW].rearrange("p (d t) -> p d t", d=8)
                return e.dma_start(out=dst, in_=src)
            extra = ABRIDGE if i == 0 else ()
            P.op("sync", relo, writes=("stg0_0", "stg0_1", "stg1_0", "stg1_1",
                                       "stg2_0", "stg2_1", "hbs", "xnfre") + extra, dma=True)
            P.op("sync", lambda e, i=i: e.dma_start(
                out=Bt[0:64, :], in_=bc_scr[0:64, i * TT:(i + 1) * TT]),
                 writes=("Bt",), dma=True)
            P.op("sync", lambda e, i=i: e.dma_start(
                out=Ct[0:64, :], in_=bc_scr[64:128, i * TT:(i + 1) * TT]),
                 writes=("Ct",), dma=True)
            P.op("sync", lambda e, i=i: e.dma_start(
                out=dtt[0:32, :], in_=dt_scr[:, i * TT:(i + 1) * TT]),
                 writes=("dtt",), dma=True)
            P.op("sync", lambda e, i=i: e.dma_start(
                out=XT[:, :].rearrange("p (g c) -> p g c", g=2),
                in_=xt_scr[i * TT:(i + 1) * TT, :].rearrange("(g p) c -> p g c", g=2)),
                 writes=("XT0", "XT1"), dma=True)
            for d in range(8):
                def xc(e, d=d):
                    dst = xnb[:, d * TW:(d + 1) * TW]
                    s_ = stg[:, d * TW:(d + 1) * TW]
                    return e.tensor_copy(dst, s_) if d % 2 == 0 else e.copy(dst, s_)
                P.op("vector" if d % 2 == 0 else "scalar", xc,
                     reads=("xnfre",), writes=(f"xnb{d}",))

        def quant_c(i):
            for d in range(8):
                P.op("scalar", lambda e, d=d: e.activation(
                    pool[:, 0:TT], stg[:, d * TW + 3:(d + 1) * TW], AF.Copy,
                    scale=rcol[:, 0:1], bias=12582912.0),
                     reads=("xnfre", "rcol", "cpool"), writes=("qt",))
                P.op("scalar", lambda e, d=d: e.activation(
                    xqs[:, d * TT:(d + 1) * TT], pool[:, 0:TT], AF.Copy,
                    bias=-12582912.0),
                     reads=("qt", "cpool"), writes=(f"xqs{d}",))
            P.op("gpsimd", lambda e: e.dma_start(out=xq_scr, in_=xqs[:, :]),
                 reads=tuple(f"xqs{d}" for d in range(8)), writes=("xqscr",), dma=True)
            P.op("gpsimd", lambda e: e.dma_start(out=xqs[:, :], in_=xq_scr),
                 reads=("xqscr",), writes=tuple(f"xqs{d}" for d in range(8)), dma=True)

        def tail_c(i, nxt):
            for y in range(16):
                sl = y % 8
                P.op("sync", lambda e, y=y, sl=sl: e.dma_start(
                    out=wktb[:, sl * 1024:(sl + 1) * 1024],
                    in_=wf_scr[y * 128:(y + 1) * 128, :]),
                     writes=(f"wkb{sl}",), dma=True)
                for qq in range(2):
                    pb = PA if qq == 0 else PB
                    pks = ("PA.0", "PA.1") if qq == 0 else ("PB.0", "PB.1")
                    fns = []
                    for j in range(2):
                        fns.append(lambda e, y=y, qq=qq, sl=sl, j=j, pb=pb: e.matmul(
                            pb[:, j * 512:(j + 1) * 512],
                            ybT[:, qq * 2048 + y * 128: qq * 2048 + (y + 1) * 128],
                            wktb[:, sl * 1024 + j * 512: sl * 1024 + (j + 1) * 512],
                            start=(y == 0), stop=False))
                    P.mms(fns, reads=(f"wkb{sl}", f"ybT{qq}"), writes=pks)
                if nxt is not None:
                    if y == 1:
                        head_c(nxt)
                    elif y == 4:
                        z_half(nxt, 0, 0)
                    elif y == 7:
                        z_half(nxt, 0, 1)
                    elif y == 10:
                        z_half(nxt, 1, 0)
                    elif y == 13:
                        z_half(nxt, 1, 1)
            for qq in range(2):
                pb = PA if qq == 0 else PB
                pks = ("PA.0", "PA.1") if qq == 0 else ("PB.0", "PB.1")
                fns = []
                for k in range(8):
                    for j in range(2):
                        fns.append(lambda e, qq=qq, k=k, j=j, pb=pb: e.matmul(
                            pb[:, j * 512:(j + 1) * 512],
                            xqs[:, k * TT + qq * 128: k * TT + (qq + 1) * 128],
                            wqtb[:, k * 1024 + j * 512: k * 1024 + (j + 1) * 512],
                            start=False, stop=False))
                for j in range(2):
                    fns.append(lambda e, j=j, pb=pb: e.matmul(
                        pb[:, j * 512:(j + 1) * 512], oneb[0:1, :],
                        fbb[0:1, j * 512:(j + 1) * 512], start=False, stop=True))
                P.mms(fns, reads=("wqtb", "fbb", "oneb") +
                      tuple(f"xqs{k}" for k in range(8)), writes=pks)
                P.op("sync", lambda e, i=i, qq=qq: e.dma_start(
                    out=resb[:, 0:1024],
                    in_=Xc[3 + i * TT + qq * 128: 3 + i * TT + (qq + 1) * 128, :]),
                     writes=("xout",), dma=True)
                P.op("vector", lambda e, pb=pb: e.tensor_add(
                    resb[:, 1024:2048], pb[:, 0:1024], resb[:, 0:1024]),
                     reads=pks + ("xout",), writes=("osb",))
                P.op("scalar", lambda e, i=i, qq=qq: e.dma_start(
                    out=OUT[i * TT + qq * 128: i * TT + (qq + 1) * 128, :],
                    in_=resb[:, 1024:2048]),
                     reads=("osb",), writes=("outd",), dma=True)
            if nxt is not None:
                scans_c(nxt, 0)

        head_c(0)
        z_proj(0, 0)
        z_proj(0, 1)
        scans_c(0, 0)
        quant_c(0)
        for i in range(c_tiles):
            wbuild(i, 0)
            if i > 0:
                quant_c(i)
            scans_c(i, 1)
            glue_c(i, 0)
            wbuild(i, 1)
            ybt_c(0)
            glue_c(i, 1)
            ybt_c(1)
            tail_c(i, i + 1 if i + 1 < c_tiles else None)

        def dump(name, src, reads):
            if name in dbg_outs:
                P.op("gpsimd", lambda e, d=dbg_outs[name], s=src: e.dma_start(out=d, in_=s),
                     reads=reads, writes=(name,), dma=True)
        dump("dbg_xnf", xnf_scr, ("xnfg16",))
        dump("dbg_xbcx", xbcx_scr, ("xbspl",))
        dump("dbg_bc", bc_scr, ("bspl", "cspl"))
        dump("dbg_dt", dt_scr, ("dtspl",))
        dump("dbg_ht", ht[0:64, :], ("ht",))
        dump("dbg_xt", xt_scr, ("xtspl",))
        dump("dbg_siluz", wstg[:, 0:2048], ("wstg",))
        dump("dbg_yb", ybs[:, :], ("ybs",))
        dump("dbg_xqs", xqs[:, :], tuple(f"xqs{d}" for d in range(8)))
        dump("dbg_rcol", rcol[:, :], ("rcol",))
        dump("dbg_xw", Xw[:, :], ("Xw",))
        dump("dbg_wts", wTs[:, :], ("wTs",))
        dump("dbg_ab", aB[:, :], ("aB",))
        dump("dbg_dsm", dsm[:, :], ("css", "csL", "csw", "csa"))
        dump("dbg_bcb", bcb[:, :], ("BTb",))
        if dbgpc is not None:
            dump("dbg_pc", dbgpc[:, :], ("dbgpc",))

        P.final_barrier("gpsimd")

        with nc.Block() as block:
            @block.sync
            def _(e):
                for f in P.steps["sync"]:
                    f(e)

            @block.vector
            def _(e):
                for f in P.steps["vector"]:
                    f(e)

            @block.scalar
            def _(e):
                for f in P.steps["scalar"]:
                    f(e)

            @block.tensor
            def _(e):
                for f in P.steps["tensor"]:
                    f(e)

            @block.gpsimd
            def _(e):
                for f in P.steps["gpsimd"]:
                    f(e)

    return nc



def make_inputs(L0_attn_X, L0_attn_in_proj_w, L0_attn_conv_w, L0_attn_conv_b,
                L0_attn_dt_bias, L0_attn_A_log, L0_attn_D, L0_attn_norm_w,
                L0_attn_out_proj_w, L0_attn_to_h_w, L0_ffn_weight_int8,
                L0_ffn_weight_scale, L0_ffn_bias):
    X = np.asarray(L0_attn_X)
    WinT = np.ascontiguousarray(np.asarray(L0_attn_in_proj_w).T)
    WoT = np.ascontiguousarray(np.asarray(L0_attn_out_proj_w).T)
    WkT = np.ascontiguousarray(np.asarray(L0_attn_to_h_w).T[:, 0:DIM])
    Wq8 = np.ascontiguousarray(np.asarray(L0_ffn_weight_int8))
    cw = np.asarray(L0_attn_conv_w).reshape(2176, 4)
    cb = np.asarray(L0_attn_conv_b)
    cwx = np.ascontiguousarray(cw[:2048].reshape(16, 128, 4).transpose(1, 0, 2).reshape(128, 64))
    cbx = np.ascontiguousarray(cb[:2048].reshape(16, 128).T)
    cwB = np.zeros((128, 4), np.float32); cwB[0:64] = cw[2048:2112]
    cbB = np.zeros((128, 1), np.float32); cbB[0:64, 0] = cb[2048:2112]
    cwB[64:128] = cw[2112:2176]
    cbB[64:128, 0] = cb[2112:2176]
    dtb = np.zeros((128, 1), np.float32); dtb[0:32, 0] = np.asarray(L0_attn_dt_bias)
    alog = np.zeros((128, 1), np.float32); alog[0:32, 0] = np.asarray(L0_attn_A_log)
    D = np.asarray(L0_attn_D)
    dpc = np.ascontiguousarray(np.broadcast_to(D[None, :], (128, 32)))
    nwv = np.asarray(L0_attn_norm_w)
    nw = np.ascontiguousarray(nwv.reshape(16, 128).T)
    wsc = np.ascontiguousarray(np.asarray(L0_ffn_weight_scale).reshape(8, 128).T)
    fb = np.ascontiguousarray(np.broadcast_to(np.asarray(L0_ffn_bias)[None, :], (128, DIM)))

    r3 = (np.arange(96)[None, :] % 32 == np.arange(32)[:, None]).astype(np.float32)
    cvals = np.zeros((128, len(CONSTVALS)), np.float32)
    for ci, cv in enumerate((1e-5, 1.0, 0.0, -1.0, -0.5, 1.0 / 2048.0, 1e-30, 0.5)):
        cvals[:, ci] = cv
    ident = np.eye(128, dtype=np.float32)
    IDb = ident
    NBI = -16384.0 * ident
    SLO = (np.arange(128)[:, None] > np.arange(128)[None, :]).astype(np.float32)
    kk = np.arange(96) % 32
    hh = np.arange(4096) // 128
    OHTv = np.zeros((128, 4096), np.float32)
    OHTv[0:96] = (kk[:, None] == hh[None, :]).astype(np.float32)
    D96v = OHTv.copy()
    ONEBv = np.ones((128, 128), np.float32)

    import ml_dtypes
    bf = lambda a: np.ascontiguousarray(a).astype(ml_dtypes.bfloat16)

    common = dict(WinT=WinT, WoT=WoT, WkT=WkT, Wq8=Wq8, wsc=wsc, fbias=fb, nw=nw,
                  cwx=cwx, cbx=cbx, cwB=cwB, cbB=cbB,
                  dtb=dtb, alog=alog, dpc=dpc,
                  IDb=bf(IDb), NBI=bf(NBI), SLO=bf(SLO), OHT=bf(OHTv),
                  ONEB=bf(ONEBv), IDf=ident, CVALS=cvals, R3F=r3)
    maps = []
    for c in range(8):
        b, h = c // 2, c % 2
        Xc = np.zeros((ST, DIM), np.float32)
        if h == 1:
            Xc[0:3] = X[b, T - 3:T, :]
        Xc[3:3 + T] = X[b, h * T:(h + 1) * T, :]
        g = np.zeros((128, 1), np.float32)
        g[:, 0] = float(h)
        m = dict(common)
        m["Xc"] = Xc
        m["gate"] = g
        maps.append(m)
    return maps


_NC_CACHE = {}


def kernel(**inputs):
    key = "full"
    if key not in _NC_CACHE:
        _NC_CACHE[key] = build_nc()
    nc = _NC_CACHE[key]
    maps = make_inputs(**inputs)
    res = run_bass_kernel_spmd(nc, maps, core_ids=list(range(8)))
    out = np.empty((4, 4096, DIM), np.float32)
    for c in range(8):
        b, h = c // 2, c % 2
        out[b, h * T:(h + 1) * T, :] = res.results[c]["OUT"]
    return out


# version 25
# speedup vs baseline: 1.0148x; 1.0148x over previous
import numpy as np
from contextlib import ExitStack

import concourse.bass as bass
import concourse.mybir as mybir
from concourse.bass_utils import run_bass_kernel_spmd

F32 = mybir.dt.float32
BF16 = mybir.dt.bfloat16
I8 = mybir.dt.int8
AL = mybir.AluOpType
AF = mybir.ActivationFunctionType
AX = mybir.AxisListType.X

DIM = 1024
DIN = 2048
NH = 32
T = 2048
ST = 2176
NT = 8
TT = 256
TW = TT + 3
Q = 128
EPS = 1e-5
CONSTVALS = (1e-5, 1.0, 0.0, -1.0, -0.5, 1.0 / 2048.0, 1e-30, 0.5, 12582912.0, -12582912.0)

ENGS = ("sync", "vector", "scalar", "tensor", "gpsimd")
ESEM = {"sync": "sy", "vector": "dve", "scalar": "act", "tensor": "pe", "gpsimd": "pool"}
DSEM = {"sync": "dS", "vector": "dV", "scalar": "dA", "tensor": "dT", "gpsimd": "dG"}


class Prog:

    def __init__(self, sems):
        self.sems = sems
        self.steps = {e: [] for e in ENGS}
        self.cnt = {s: 0 for s in sems}
        self.water = {e: {s: 0 for s in sems} for e in ENGS}
        self.last_w = {}
        self.readers = {}
        self.dmaidx = {q: 0 for q in ("dS", "dV", "dA", "dG")}

    ELIDE = True
    ELIDE_ENGS = ("scalar", "tensor")

    def _waits(self, eng, reads, writes):
        own = ESEM[eng] if (Prog.ELIDE and eng in Prog.ELIDE_ENGS) else "-"
        need = {}
        for r in tuple(reads) + tuple(writes):
            lw = self.last_w.get(r)
            if lw and lw[0] != own and self.water[eng].get(lw[0], 0) < lw[1]:
                need[lw[0]] = max(need.get(lw[0], 0), lw[1])
        for r in writes:
            for s, v in self.readers.get(r, ()):
                if s != own and self.water[eng].get(s, 0) < v:
                    need[s] = max(need.get(s, 0), v)
        return sorted(need.items())

    def _book(self, eng, waits, semname, val, reads, writes):
        for s, v in waits:
            self.water[eng][s] = max(self.water[eng][s], v)
        for r in writes:
            self.last_w[r] = (semname, val)
            self.readers[r] = []
        for r in reads:
            self.readers.setdefault(r, []).append((semname, val))

    SERIAL_DMA = False
    DMA_W = 8

    def op(self, eng, fn, reads=(), writes=(), dma=False, sem=None):
        waits = self._waits(eng, reads, writes)
        if dma:
            q = DSEM[eng]
            slot = self.dmaidx[q] % self.DMA_W
            self.dmaidx[q] += 1
            semname = f"{q}{slot}"
            amt = 16
            if self.SERIAL_DMA:
                prevq = f"{q}{(slot - 1) % self.DMA_W}"
                w = dict(waits)
                if self.cnt.get(prevq, 0) > 0:
                    w[prevq] = max(w.get(prevq, 0), self.cnt[prevq])
                waits = sorted(w.items())
            if self.cnt[semname] > 0:
                w = dict(waits)
                if self.water[eng].get(semname, 0) < self.cnt[semname]:
                    w[semname] = max(w.get(semname, 0), self.cnt[semname])
                waits = sorted(w.items())
        else:
            semname = sem or ESEM[eng]
            amt = 1
        self.cnt[semname] += amt
        val = self.cnt[semname]
        sems = self.sems

        def emit(e, fn=fn, waits=waits, semname=semname, amt=amt):
            for s, v in waits:
                e.wait_ge(sems[s], v)
            fn(e).then_inc(sems[semname], amt)

        self.steps[eng].append(emit)
        self._book(eng, waits, semname, val, reads, writes)

    def mms(self, fns, reads=(), writes=()):
        waits = self._waits("tensor", reads, writes)
        self.cnt["pe"] += 1
        val = self.cnt["pe"]
        sems = self.sems

        def emit(e, fns=fns, waits=waits):
            for s, v in waits:
                e.wait_ge(sems[s], v)
            last = None
            for f in fns:
                last = f(e)
            last.then_inc(sems["pe"], 1)

        self.steps["tensor"].append(emit)
        self._book("tensor", waits, "pe", val, reads, writes)

    def final_barrier(self, eng):
        sems = self.sems
        vals = [(s, v) for s, v in self.cnt.items() if v > 0]

        def emit(e, vals=vals):
            for s, v in vals:
                e.wait_ge(sems[s], v)

        self.steps[eng].append(emit)


def rep_h(ap2d, nh, nw):
    return ap2d.unsqueeze(2).broadcast_to([ap2d.shape[0], nh, nw])


def rep_t(ap2d, nh):
    return ap2d.unsqueeze(1).broadcast_to([ap2d.shape[0], nh, ap2d.shape[1]])


def build_nc(a_tiles=NT, c_tiles=NT, dbg=(), detect_races=False):
    nc = bass.Bass("TRN2", target_bir_lowering=False, debug=False,
                   detect_race_conditions=detect_races)

    din = {}
    def I(name, shape, dtype=F32):
        din[name] = nc.dram_tensor(name, shape, dtype, kind="ExternalInput").ap()
        return din[name]

    Xc = I("Xc", [ST, DIM])
    WinT = I("WinT", [DIM, 4256])
    WoT = I("WoT", [DIN, DIM])
    WkT = I("WkT", [DIM, DIM])
    Wq8 = I("Wq8", [DIM, DIM], I8)
    wsc = I("wsc", [128, 8])
    fbias = I("fbias", [128, DIM])
    nw = I("nw", [128, 16])
    cwx = I("cwx", [128, 64])
    cbx = I("cbx", [128, 16])
    cwB = I("cwB", [128, 4]); cbB = I("cbB", [128, 1])
    dtbi = I("dtb", [128, 1])
    alog = I("alog", [128, 1])
    dpc = I("dpc", [128, 32])
    gate = I("gate", [128, 1])
    IDb = I("IDb", [128, 128], BF16)
    NBI = I("NBI", [128, 128], BF16)
    SLO = I("SLO", [128, 128], BF16)
    OHT = I("OHT", [128, 4096], BF16)
    ONEB = I("ONEB", [128, 128], BF16)
    CVALS = I("CVALS", [128, len(CONSTVALS)])
    R3F = I("R3F", [32, 96])
    IDf = I("IDf", [128, 128], F32)

    OUT = nc.dram_tensor("OUT", [T, DIM], F32, kind="ExternalOutput").ap()

    xnf_scr = nc.dram_tensor("xnf_scr", [DIM, ST], F32, kind="Internal").ap()
    xbcx_scr = nc.dram_tensor("xbcx_scr", [DIN, T], F32, kind="Internal").ap()
    bc_scr = nc.dram_tensor("bc_scr", [128, T], F32, kind="Internal").ap()
    dt_scr = nc.dram_tensor("dt_scr", [32, T], F32, kind="Internal").ap()
    xt_scr = nc.dram_tensor("xt_scr", [T, DIN], BF16, kind="Internal").ap()
    wf_scr = nc.dram_tensor("wf_scr", [DIN, DIM], BF16, kind="Internal").ap()
    mx_in = nc.dram_tensor("mx_in", [128, 1], F32, kind="Internal").ap()
    mx_out = nc.dram_tensor("mx_out", [128, 1], F32, kind="Internal").ap()
    ag_in = nc.dram_tensor("ag_in", [64, DIN], F32, kind="Internal").ap()
    ag_out = nc.dram_tensor("ag_out", [128, DIN], F32, kind="Internal").ap()
    xq_scr = nc.dram_tensor("xq_scr", [128, 8 * TT], BF16, kind="Internal").ap()

    dbg_specs = {
        "dbg_xnf": ([DIM, ST], F32), "dbg_xbcx": ([DIN, T], F32),
        "dbg_bc": ([128, T], F32), "dbg_dt": ([32, T], F32),
        "dbg_ht": ([64, DIN], F32), "dbg_xt": ([T, DIN], BF16),
        "dbg_siluz": ([128, DIN], F32), "dbg_yb": ([128, DIN], BF16),
        "dbg_xqs": ([128, 8 * TT], BF16), "dbg_rcol": ([128, 2], F32),
        "dbg_xw": ([128, DIN], BF16), "dbg_wts": ([128, 34], F32),
        "dbg_ab": ([128, 32], F32), "dbg_dsm": ([128, 2 * TT + 7 * Q + 272], F32),
        "dbg_bcb": ([128, 2 * Q + 64], BF16),
        "dbg_pc": ([128, 2048], BF16),
    }
    dbg_outs = {n: nc.dram_tensor(n, s, d, kind="ExternalOutput").ap()
                for n, (s, d) in dbg_specs.items() if n in dbg}

    with ExitStack() as ctx:
        def sb(name, f, dtype=F32, p=128):
            return ctx.enter_context(nc.sbuf_tensor(name, [p, f], dtype))

        idb = sb("idb", 128, BF16)
        nbi = sb("nbi", 128, BF16)
        slo = sb("slo", 128, BF16)
        oneb = sb("oneb", 128, BF16)
        idf = sb("idf", 128)
        oht = sb("oht", 4096, BF16)
        cwxs = sb("cwxs", 64); cbxs = sb("cbxs", 16)
        cwBs = sb("cwBs", 4); cbBs = sb("cbBs", 1)
        dtbs = sb("dtbs", 1)
        negA = sb("negA", 1)
        dpcs = sb("dpcs", 32)
        dpcb = sb("dpcb", 32, BF16)
        wTb = sb("wTb", 64, BF16)
        gats = sb("gats", 1)
        nws = sb("nws", 16)
        wscs = sb("wscs", 8)
        fbb = sb("fbb", DIM, BF16)
        wintb = sb("wintb", 8 * 2208, BF16)
        wktb = sb("wktb", 8 * 1024, BF16)
        wqtb = sb("wqtb", 8 * 1024, BF16)
        wstg = sb("wstg", 2208)
        w8i = sb("w8i", 2048, I8)
        ht = sb("ht", DIN)
        htb = sb("htb", DIN, BF16)
        stg = sb("stg", 3 * DIM)
        xnb = sb("xnb", 8 * TW, BF16)
        mxc = sb("mxc", 2)
        mvst = sb("mvst", 16)
        xbcb = sb("xbcb", 2048, BF16)
        Bt = sb("Bt", TT)
        Ct = sb("Ct", TT)
        dtt = sb("dtt", TT)
        dsm = sb("dsm", 2 * TT + 7 * Q + 264 + 8)
        sHL = sb("sHL", 2 * Q, BF16)
        LHL = sb("LHL", 2 * Q, BF16)
        bcb = sb("bcb", 2 * (2 * Q + 64), BF16)
        wTs = sb("wTs", 34)
        esT = sb("esT", 64)
        aB = sb("aB", 64)
        XT = sb("XT", 2 * DIN, BF16)
        Xw = sb("Xw", DIN, BF16)
        ssq = sb("ssq", 4)
        rcol = sb("rcol", 2)
        mrow = sb("mrow", 360)
        cpool = sb("cpool", len(CONSTVALS))
        resb = sb("resb", 2048)
        r3f = sb("r3f", 96, p=32)

        dbgpc = sb("dbgpc", 2048, BF16) if "dbg_pc" in dbg_outs else None
        actx = ExitStack()
        def sbA(name, f, dtype=F32, p=128):
            return actx.enter_context(nc.sbuf_tensor(name, [p, f], dtype))
        xg = sbA("xg", DIM)
        xnt = sbA("xnt", DIM)
        xbcx = sbA("xbcx", 16 * TT)
        cvt = sbA("cvt", 4 * TT)

        sems = {}
        semnames = ["pe", "dve", "act", "pool", "sy", "cc"]
        for q in ("dS", "dV", "dA", "dG"):
            for w in range(Prog.DMA_W):
                semnames.append(f"{q}{w}")
        for s in semnames:
            sems[s] = ctx.enter_context(nc.semaphore(s))

        PA = ctx.enter_context(nc.psum_tensor("PA", [128, 1024], F32))
        PB = ctx.enter_context(nc.psum_tensor("PB", [128, 1024], F32))
        PC = ctx.enter_context(nc.psum_tensor("PC", [128, 1024], F32))
        PS = ctx.enter_context(nc.psum_tensor("PS", [128, 512], F32))
        PT = ctx.enter_context(nc.psum_tensor("PT", [128, 1024], BF16))

        P = Prog(sems)
        dbgflag = {"done": False}

        def ld(dst, src, key, eng="sync"):
            keys = key if isinstance(key, tuple) else (key,)
            P.op(eng, lambda e, d=dst, s=src: e.dma_start(out=d, in_=s),
                 writes=keys, dma=True)

        for t_, s_, k_ in [(idb, IDb, "idb"), (nbi, NBI, "nbi"), (slo, SLO, "slo"),
                           (oneb, ONEB, "oneb"), (idf, IDf, "idf"), (oht, OHT, "oht"),
                           (cwxs, cwx, "cwxs"), (cbxs, cbx, "cbxs"),
                           (cwBs, cwB, "cwBs"), (cbBs, cbB, "cbBs"),
                           (dtbs, dtbi, "dtbs"), (dpcs, dpc, "dpcs"),
                           (gats, gate, "gats"), (nws, nw, "nws"),
                           (wscs, wsc, "wscs"), (cpool, CVALS, "cpool"), (r3f, R3F, "r3f"),
                           (negA, alog, "negAr")]:
            ld(t_[:, :], s_, k_)
        for ci, cv in enumerate(CONSTVALS):
            nc.const_aps.register(F32, cv, cpool[:, ci:ci + 1])
        ld(wstg[:, 0:1024], fbias, "wstg")
        P.op("vector", lambda e: e.tensor_copy(fbb[:, :], wstg[:, 0:1024]),
             reads=("wstg",), writes=("fbb",))
        P.op("scalar", lambda e: e.activation(negA[:, :], negA[:, :], AF.Exp),
             reads=("negAr", "cpool"), writes=("negA1",))
        P.op("scalar", lambda e: e.mul(negA[:, :], negA[:, :], -1.0),
             reads=("negA1",), writes=("negA",))
        P.op("vector", lambda e: e.memset(ht[0:64, :], 0.0), writes=("ht",))
        P.op("vector", lambda e: e.memset(mxc[:, :], 0.0), writes=("mxc",))
        P.op("vector", lambda e: e.memset(mrow[0:1, 32:224], 1.0), writes=("onesr",))
        P.op("vector", lambda e: e.tensor_copy(dpcb[:, :], dpcs[:, :]),
             reads=("dpcs",), writes=("dpcb",))

        STGALL = ("stg0_0", "stg0_1", "stg1_0", "stg1_1", "stg2_0", "stg2_1")
        XB9 = tuple(f"xbcx{c}" for c in range(9))
        SLOT3 = [(lambda w: wstg[:, 0:w], ("wstg",)),
                 (lambda w: stg[:, 0:w], STGALL),
                 (lambda w: xbcx[:, 0:w], XB9)]
        SLOT5 = [(lambda w: wstg[:, 0:w], ("wstg",)),
                 (lambda w: stg[:, 0:w], ("stg0_0", "stg0_1")),
                 (lambda w: stg[:, 1024:1024 + w], ("stg1_0", "stg1_1")),
                 (lambda w: xbcx[:, 0:w], tuple(f"xbcx{c}" for c in range(4))),
                 (lambda w: xbcx[:, 1024:1024 + w], tuple(f"xbcx{c}" for c in range(4, 8)))]
        CENG = ["vector", "scalar", "vector"]
        for d in range(8):
            slf, keys = SLOT3[d % 3]
            sl = slf(2208)
            ld(sl, WinT[d * 128:(d + 1) * 128, 2048:4256], keys)
            eng = CENG[d % 3]
            if eng == "scalar":
                P.op("scalar", lambda e, d=d, sl=sl: e.copy(
                    wintb[:, d * 2208:(d + 1) * 2208], sl),
                     reads=keys, writes=(f"wintb{d}",))
            else:
                P.op(eng, lambda e, d=d, sl=sl: e.tensor_copy(
                    wintb[:, d * 2208:(d + 1) * 2208], sl),
                     reads=keys, writes=(f"wintb{d}",))
        for d in range(8):
            slf, keys = SLOT5[d % 5]
            sl = slf(1024)
            ld(sl, WkT[d * 128:(d + 1) * 128, :], keys)
            eng = CENG[d % 3]
            if eng == "scalar":
                P.op("scalar", lambda e, d=d, sl=sl: e.copy(
                    wktb[:, d * 1024:(d + 1) * 1024], sl),
                     reads=keys, writes=(f"wkb{d}",))
            else:
                P.op(eng, lambda e, d=d, sl=sl: e.tensor_copy(
                    wktb[:, d * 1024:(d + 1) * 1024], sl),
                     reads=keys, writes=(f"wkb{d}",))

        def ln_group(g):
            gs = ""
            xg_s = xg[:, :]
            xn_s = xnt[:, :]
            ld(xg_s, Xc[g * 128:(g + 1) * 128, :], "xg")
            P.op("vector", lambda e, xg_s=xg_s: e.bn_stats(mvst[:, 0:6], xg_s[:, 0:512]),
                 reads=("xg",), writes=("bn6",))
            P.op("vector", lambda e, xg_s=xg_s: e.bn_stats(mvst[:, 6:12], xg_s[:, 512:1024]),
                 reads=("xg",), writes=("bn6",))
            P.op("vector", lambda e: e.bn_aggr(mvst[:, 12:14], mvst[:, 0:12]),
                 reads=("bn6",), writes=("bn2",))
            P.op("scalar", lambda e: e.activation(mvst[:, 14:15], mvst[:, 13:14], AF.Ln, bias=EPS),
                 reads=("bn2",), writes=("lnv",))
            P.op("scalar", lambda e: e.activation(mvst[:, 14:15], mvst[:, 14:15], AF.Exp, scale=-0.5),
                 reads=("lnv",), writes=("rsv",))
            P.op("vector", lambda e, xg_s=xg_s, xn_s=xn_s: e.tensor_scalar(
                xn_s, xg_s, mvst[:, 12:13], mvst[:, 14:15], AL.subtract, AL.mult),
                 reads=("xg", "bn2", "rsv"), writes=("xnt",))
            P.op("vector", lambda e, xn_s=xn_s: e.tensor_reduce(
                mxc[:, 1:2], xn_s, AX, AL.max, apply_absolute_value=True),
                 reads=("xnt",), writes=("mxg",))
            P.op("vector", lambda e: e.tensor_max(mxc[:, 0:1], mxc[:, 0:1], mxc[:, 1:2]),
                 reads=("mxg", "mxc"), writes=("mxc",))
            ss = g % 3
            for half in range(2):
                pbank = PS if half == 0 else PC
                pkey = "PS.0" if half == 0 else "PC.0"
                fns = []
                for j in range(4):
                    d = half * 4 + j
                    fns.append(lambda e, d=d, j=j, xn_s=xn_s, pbank=pbank: e.transpose(
                        pbank[:, j * 128:(j + 1) * 128], xn_s[:, d * 128:(d + 1) * 128], idf[:, :]))
                P.mms(fns, reads=("xnt", "idf"), writes=(pkey,))
                eng = "vector" if half == 0 else "scalar"
                def ev(e, half=half, ss=ss, eng=eng, pbank=pbank):
                    dst = stg[:, ss * DIM + half * 512: ss * DIM + (half + 1) * 512]
                    return e.tensor_copy(dst, pbank[:, 0:512]) if eng == "vector" \
                        else e.copy(dst, pbank[:, 0:512])
                P.op(eng, ev, reads=(pkey,), writes=(f"stg{ss}_{half}",))
            def spill(e, g=g, ss=ss):
                src = stg[:, ss * DIM:(ss + 1) * DIM].rearrange("p (d t) -> p d t", d=8)
                dst = xnf_scr.rearrange("(d p) t -> p d t", d=8)[:, :, g * 128:(g + 1) * 128]
                return e.dma_start(out=dst, in_=src)
            P.op("scalar", spill, reads=(f"stg{ss}_0", f"stg{ss}_1"),
                 writes=(f"xnfg{g}",), dma=True)

        def xnb_build(i):
            lo = i * 256
            off = 0
            while off < TW:
                g = (lo + off) // 128
                wi = (lo + off) % 128
                tk = min(128 - wi, TW - off)
                ss = g % 3
                for d in range(8):
                    def xc(e, d=d, ss=ss, wi=wi, off=off, tk=tk):
                        dst = xnb[:, d * TW + off: d * TW + off + tk]
                        s_ = stg[:, ss * DIM + d * 128 + wi: ss * DIM + d * 128 + wi + tk]
                        return e.tensor_copy(dst, s_) if d % 2 == 0 else e.copy(dst, s_)
                    P.op("vector" if d % 2 == 0 else "scalar", xc,
                         reads=(f"stg{ss}_{0 if d < 4 else 1}",), writes=(f"xnb{d}",))
                off += tk

        XNBKEYS = tuple(f"xnb{d}" for d in range(8))
        WINKEYS = tuple(f"wintb{d}" for d in range(8))

        def conv_tap(kind, c, k, psrc, pkeys):
            sl = c % 4
            wv = cwxs[:, c * 4:c * 4 + 4] if kind == "x" else cwBs[:, :]
            acc = cvt[:, sl * TT:(sl + 1) * TT]
            if k == 0:
                P.op("scalar", lambda e, acc=acc, psrc=psrc, wv=wv: e.activation(
                    acc, psrc[:, 0:TT], AF.Copy, scale=wv[:, 0:1]),
                     reads=pkeys, writes=(f"cacc{sl}",))
            else:
                P.op("vector", lambda e, acc=acc, psrc=psrc, wv=wv, k=k: e.scalar_tensor_tensor(
                    acc, psrc[:, k:k + TT], wv[:, k:k + 1], acc, AL.mult, AL.add),
                     reads=pkeys + (f"cacc{sl}",), writes=(f"cacc{sl}",))

        def conv_silu(kind, c):
            sl = c % 4
            acc = cvt[:, sl * TT:(sl + 1) * TT]
            if kind == "x":
                bv = cbxs[:, c:c + 1]
                dst, dkey = xbcx[:, c * TT:(c + 1) * TT], f"xbcx{c}"
            else:
                bv = cbBs[:, 0:1]
                dst, dkey = Bt[:, :], "Bt"
            P.op("scalar", lambda e, dst=dst, acc=acc, bv=bv: e.activation(
                dst, acc, AF.Silu, bias=bv),
                 reads=(f"cacc{sl}",), writes=(dkey,))

        def inproj_A():
            banks = [(PA, 0, "PA.0"), (PA, 512, "PA.1"), (PB, 0, "PB.0"), (PB, 512, "PB.1")]
            def pair_srcs(p):
                out = []
                for c in (2 * p, 2 * p + 1):
                    pb, off, pk = banks[c % 4]
                    out.append((c, pb[:, off:off + TW], (pk,)))
                return out

            prev = None
            for p in range(8):
                if p >= 2:
                    conv_silu("x", 2 * p - 4)
                    conv_silu("x", 2 * p - 3)
                for c in (2 * p, 2 * p + 1):
                    pb, off, pk = banks[c % 4]
                    fns = []
                    for d in range(8):
                        fns.append(lambda e, c=c, d=d, pb=pb, off=off: e.matmul(
                            pb[:, off:off + TW],
                            wintb[:, d * 2208 + c * 128: d * 2208 + (c + 1) * 128],
                            xnb[:, d * TW:(d + 1) * TW], start=(d == 0), stop=(d == 7)))
                    P.mms(fns, reads=WINKEYS + XNBKEYS, writes=(pk,))
                    conv_tap("x", c, 0, pb[:, off:off + TW], (pk,))
                if prev is not None:
                    for k in (1, 2, 3):
                        for c, psrc, pks_ in prev:
                            conv_tap("x", c, k, psrc, pks_)
                prev = pair_srcs(p)
            conv_silu("x", 12)
            conv_silu("x", 13)
            fns = []
            for d in range(8):
                fns.append(lambda e, d=d: e.matmul(
                    PC[:, 0:TW],
                    wintb[:, d * 2208 + 2048: d * 2208 + 2176],
                    xnb[:, d * TW:(d + 1) * TW], start=(d == 0), stop=(d == 7)))
            P.mms(fns, reads=WINKEYS + XNBKEYS, writes=("PC.0",))
            conv_tap("BC", 17, 0, PC[:, 0:TW], ("PC.0",))
            for k in (1, 2, 3):
                for c, psrc, pks_ in prev:
                    conv_tap("x", c, k, psrc, pks_)
            conv_silu("x", 14)
            conv_silu("x", 15)
            for k in (1, 2, 3):
                conv_tap("BC", 17, k, PC[:, 0:TW], ("PC.0",))
            conv_silu("BC", 17)
            fns = []
            for d in range(8):
                fns.append(lambda e, d=d: e.matmul(
                    PS[0:32, 0:TW],
                    wintb[:, d * 2208 + 2176: d * 2208 + 2208],
                    xnb[:, d * TW:(d + 1) * TW], start=(d == 0), stop=(d == 7)))
            P.mms(fns, reads=WINKEYS + XNBKEYS, writes=("PS.0",))
            u = dsm[0:32, 0:TT]; au = dsm[0:32, TT:2 * TT]
            P.op("vector", lambda e: e.tensor_scalar(u, PS[0:32, 3:TW], dtbs[0:32, :],
                                                     None, AL.add),
                 reads=("PS.0", "dtbs"), writes=("spu",))
            P.op("scalar", lambda e: e.activation(au, u, AF.Abs),
                 reads=("spu",), writes=("spau",))
            P.op("scalar", lambda e: e.activation(au, au, AF.Exp, scale=-1.0),
                 reads=("spau",), writes=("spe",))
            P.op("scalar", lambda e: e.activation(au, au, AF.Ln, bias=1.0),
                 reads=("spe",), writes=("spl",))
            ur = dsm[0:32, 2 * TT + 7 * Q + 8:2 * TT + 7 * Q + 8 + TT]
            P.op("vector", lambda e, ur=ur: e.tensor_scalar_max(ur, u, 0.0),
                 reads=("spu",), writes=("spr",))
            P.op("vector", lambda e, ur=ur: e.tensor_add(dtt[0:32, :], au, ur),
                 reads=("spl", "spr"), writes=("dtt",))

        XBCXKEYS = tuple(f"xbcx{c}" for c in range(16))

        def xt_build():
            for gg in range(2):
                src = xbcx[:, :].rearrange("p (c t) -> p c t", c=16)[:, :, gg * 128:(gg + 1) * 128]
                dst = xbcb[:, 0:2048].rearrange("p (c t) -> p c t", c=16)
                P.op("scalar", lambda e, src=src, dst=dst: e.copy(dst, src),
                     reads=XBCXKEYS, writes=("xbcb",))
                for hb in range(2):
                    fns = []
                    for j in range(8):
                        c = hb * 8 + j
                        fns.append(lambda e, c=c, j=j: e.transpose(
                            PT[:, j * 128:(j + 1) * 128], xbcb[:, c * 128:(c + 1) * 128], idb[:, :]))
                    P.mms(fns, reads=("xbcb", "idb"), writes=("PT.0",))
                    P.op("vector", lambda e, gg=gg, hb=hb: e.tensor_copy(
                        XT[:, gg * DIN + hb * 1024: gg * DIN + (hb + 1) * 1024], PT[:, :]),
                         reads=("PT.0",), writes=(f"XT{gg}",))

        def chunk_small(qq, phase):
            c0 = qq * Q
            o = 2 * TT
            a = dsm[0:32, o:o + Q]
            s = dsm[0:32, o + Q:o + 2 * Q]
            L = dsm[0:32, o + 2 * Q:o + 3 * Q]
            lnd = dsm[0:32, o + 3 * Q:o + 4 * Q]
            wv = dsm[0:32, o + 4 * Q:o + 5 * Q]
            alc = dsm[0:32, o + 5 * Q:o + 5 * Q + 1]
            P.op("vector", lambda e, a=a, c0=c0: e.tensor_scalar(
                a, dtt[0:32, c0:c0 + Q], negA[0:32, :], None, AL.mult),
                 reads=("dtt", "negA"), writes=("csa",))
            P.op("vector", lambda e, s=s, a=a: e.tensor_tensor_scan(
                s, cpool[0:32, 1:2].broadcast_to([32, Q]), a, 0.0, AL.mult, AL.add),
                 reads=("csa", "cpool"), writes=("css",))
            P.op("vector", lambda e, lnd=lnd, c0=c0: e.tensor_scalar_max(
                lnd, dtt[0:32, c0:c0 + Q], 1e-30),
                 reads=("dtt",), writes=("csd",))
            P.op("scalar", lambda e, lnd=lnd: e.activation(lnd, lnd, AF.Ln),
                 reads=("csd",), writes=("cslnd",))
            P.op("vector", lambda e, L=L, lnd=lnd, s=s: e.tensor_sub(L, lnd, s),
                 reads=("cslnd", "css"), writes=("csL",))
            P.op("vector", lambda e, wv=wv, L=L, s=s: e.tensor_scalar(
                wv, L, s[:, Q - 1:Q], None, AL.add),
                 reads=("csL", "css"), writes=("cswa",))
            P.op("scalar", lambda e, wv=wv: e.activation(wv, wv, AF.Exp),
                 reads=("cswa",), writes=("csw",))
            P.mms([lambda e, wv=wv: e.transpose(PS[0:128, 128:160], wv, idf[0:32, 0:32])],
                  reads=("csw", "idf"), writes=("PS.0",))
            P.op("vector", lambda e, qq=qq: e.tensor_copy(
                wTb[:, qq * 32:qq * 32 + 32], PS[:, 128:160]),
                 reads=("PS.0",), writes=(f"wTb{qq}",))
            P.op("scalar", lambda e, alc=alc, s=s: e.activation(alc, s[:, Q - 1:Q], AF.Exp),
                 reads=("css",), writes=("csal",))
            P.mms([lambda e, alc=alc: e.transpose(PS[0:1, 160:192], alc, idf[0:32, 0:32])],
                  reads=("csal", "idf"), writes=("PS.0",))
            P.op("vector", lambda e: e.tensor_copy(mrow[0:1, 0:32], PS[0:1, 160:192]),
                 reads=("PS.0",), writes=("alrow",))
            P.mms([lambda e: e.matmul(PS[0:64, 192:224], mrow[0:1, 32:96],
                                      mrow[0:1, 0:32], start=True, stop=True)],
                  reads=("onesr", "alrow"), writes=("PS.0",))
            P.op("vector", lambda e, qq=qq: e.tensor_copy(
                aB[0:64, qq * 32:qq * 32 + 32], PS[0:64, 192:224]),
                 reads=("PS.0",), writes=(f"aB{qq}",))
            if phase == "C":
                P.mms([lambda e, s=s: e.matmul(PS[0:64, 128:384], r3f[:, 0:64],
                                               dsm[0:32, o + Q:o + 3 * Q],
                                               start=True, stop=True)],
                      reads=("r3f", "css", "csL"), writes=("PS.0",))
                t96 = dsm[0:96, o + 5 * Q + 8:o + 7 * Q + 8]
                for li, (dstT0, kb) in enumerate(((sHL, f"sHL{qq}"), (LHL, f"LHL{qq}"))):
                    dstT = dstT0[:, qq * Q:(qq + 1) * Q]
                    ps = PS[0:64, 128 + li * Q:128 + (li + 1) * Q]
                    tl = t96[:, li * Q:(li + 1) * Q]
                    P.op("vector", lambda e, dstT=dstT, ps=ps: e.tensor_copy(dstT[0:64, :], ps),
                         reads=("PS.0",), writes=(kb + "_0",))
                    P.op("vector", lambda e, dstT=dstT, ps=ps, tl=tl: e.tensor_tensor(
                        tl[32:64, :], ps[32:64, :], dstT[32:64, :], AL.subtract),
                         reads=("PS.0", kb + "_0"), writes=(kb + "_t",))
                    P.op("vector", lambda e, dstT=dstT, tl=tl: e.tensor_copy(
                        dstT[32:64, :], tl[32:64, :]),
                         reads=(kb + "_t",), writes=(kb + "_1",))
                P.mms([lambda e, s=s: e.transpose(PS[0:128, 384:416], s, idf[0:32, 0:32])],
                      reads=("css", "idf"), writes=("PS.0",))
                P.op("scalar", lambda e, qq=qq: e.activation(
                    esT[:, qq * 32:(qq + 1) * 32], PS[:, 384:416], AF.Exp),
                     reads=("PS.0",), writes=(f"esT{qq}",))

        def bc_cast(qq, phase):
            c0 = qq * Q
            bo = qq * 320
            P.op("vector", lambda e, c0=c0, bo=bo: e.tensor_copy(
                bcb[0:64, bo:bo + Q], Bt[0:64, c0:c0 + Q]),
                 reads=("Bt",), writes=(f"Bb{qq}",))
            P.mms([lambda e, bo=bo: e.transpose(PT[0:128, 0:64], bcb[0:64, bo:bo + Q],
                                                idb[0:64, 0:64])],
                  reads=(f"Bb{qq}", "idb"), writes=("PT.0",))
            P.op("vector", lambda e, bo=bo: e.tensor_copy(
                bcb[:, bo + 2 * Q:bo + 2 * Q + 64], PT[:, 0:64]),
                 reads=("PT.0",), writes=(f"BTb{qq}",))
            if phase == "C":
                P.op("vector", lambda e, c0=c0, bo=bo: e.tensor_copy(
                    bcb[0:64, bo + Q:bo + 2 * Q], Ct[0:64, c0:c0 + Q]),
                     reads=("Ct",), writes=(f"Cb{qq}",))

        def xw_and_contrib(qq, phase):
            XTq = XT[:, qq * DIN:(qq + 1) * DIN]
            xtk = f"XT{qq}"
            P.op("gpsimd", lambda e, qq=qq, XTq=XTq: e.tensor_tensor(
                Xw[:, :], XTq, rep_h(wTb[:, qq * 32:qq * 32 + 32], 32, 64), AL.mult),
                 reads=(xtk, f"wTb{qq}"), writes=("Xw",))
            P.op("gpsimd", lambda e, qq=qq: e.tensor_tensor(
                ht[0:64, :], ht[0:64, :],
                rep_h(aB[0:64, qq * 32:qq * 32 + 32], 32, 64), AL.mult),
                 reads=("ht", f"aB{qq}"), writes=("ht",))
            for half in range(2):
                fns = []
                for j in range(2):
                    fns.append(lambda e, qq=qq, half=half, j=j: e.matmul(
                        PC[0:64, j * 512:(j + 1) * 512],
                        bcb[:, qq * 320 + 2 * Q: qq * 320 + 2 * Q + 64],
                        Xw[:, half * 1024 + j * 512: half * 1024 + (j + 1) * 512],
                        start=True, stop=True))
                P.mms(fns, reads=(f"BTb{qq}", "Xw"), writes=("PC.0", "PC.1"))
                P.op("vector", lambda e, half=half: e.tensor_add(
                    ht[0:64, half * 1024:(half + 1) * 1024],
                    ht[0:64, half * 1024:(half + 1) * 1024], PC[0:64, 0:1024]),
                     reads=("ht", "PC.0", "PC.1"), writes=("ht",))

        ln_group(0)
        for i in range(a_tiles):
            ln_group(2 * i + 1)
            ln_group(2 * i + 2)
            xnb_build(i)
            inproj_A()
            xt_build()
            for qq in range(2):
                chunk_small(qq, "A")
                bc_cast(qq, "A")
                xw_and_contrib(qq, "A")
            if "dbg_xbcx" in dbg_outs:
                P.op("scalar", lambda e, i=i: e.dma_start(
                    out=xbcx_scr.rearrange("(c p) t -> p c t", c=16)[:, :, i * TT:(i + 1) * TT],
                    in_=xbcx[:, :].rearrange("p (c t) -> p c t", c=16)),
                     reads=XBCXKEYS, writes=("xbspl",), dma=True)
            P.op("scalar", lambda e, i=i: e.dma_start(
                out=bc_scr[:, i * TT:(i + 1) * TT], in_=Bt[:, :]),
                 reads=("Bt",), writes=("bspl",), dma=True)
            P.op("scalar", lambda e, i=i: e.dma_start(
                out=dt_scr[:, i * TT:(i + 1) * TT], in_=dtt[0:32, :]),
                 reads=("dtt",), writes=("dtspl",), dma=True)
            P.op("scalar", lambda e, i=i: e.dma_start(
                out=xt_scr[i * TT:(i + 1) * TT, :].rearrange("(g p) c -> p g c", g=2),
                in_=XT[:, :].rearrange("p (g c) -> p g c", g=2)),
                 reads=("XT0", "XT1"), writes=("xtspl",), dma=True)

        P.op("gpsimd", lambda e: e.dma_start(out=mx_in, in_=mxc[:, 0:1]),
             reads=("mxc",), writes=("mx_in",), dma=True)
        P.op("gpsimd", lambda e: e.collective_compute(
            "AllReduce", AL.max, replica_groups=[list(range(8))],
            ins=[mx_in], outs=[mx_out]),
             reads=("mx_in",), writes=("mx_out",), sem="cc")
        P.op("gpsimd", lambda e: e.dma_start(out=ag_in, in_=ht[0:64, :]),
             reads=("ht",), writes=("ag_in",), dma=True)
        P.op("gpsimd", lambda e: e.collective_compute(
            "AllGather", AL.bypass, replica_groups=[[0, 1], [2, 3], [4, 5], [6, 7]],
            ins=[ag_in], outs=[ag_out]),
             reads=("ag_in",), writes=("ag_out",), sem="cc")
        P.op("gpsimd", lambda e: e.dma_start(out=mxc[:, 1:2], in_=mx_out),
             reads=("mx_out",), writes=("mxcc",), dma=True)
        WSLT = [(xg[:, :], ("xg",)), (xnt[:, :], ("xnt",))]
        CSLT = [(xnb[:, 0:1024], ("xnb0", "xnb1", "xnb2", "xnb3")),
                (XT[:, 1024:2048], ("XT0",))]
        MSLT = [(xnb[:, 1024:2048], ("xnb4", "xnb5", "xnb6", "xnb7")),
                (XT[:, 0:1024], ("XT0",))]
        ESLT = [(htb[:, 0:1024], ("htb",)), (htb[:, 1024:2048], ("htb",))]

        ZSLOT = [(wstg[:, 0:2048], ("wstg",)),
                 (resb[:, 0:2048], ("xout", "osb"))]

        def zreload(d):
            sl, keys = ZSLOT[d % 2]
            ld(sl, WinT[d * 128:(d + 1) * 128, 0:2048], keys)
            if d % 2 == 0:
                P.op("vector", lambda e, d=d, sl=sl: e.tensor_copy(
                    wintb[:, d * 2208:d * 2208 + 2048], sl),
                     reads=keys, writes=(f"wintb{d}",))
            else:
                P.op("scalar", lambda e, d=d, sl=sl: e.copy(
                    wintb[:, d * 2208:d * 2208 + 2048], sl),
                     reads=keys, writes=(f"wintb{d}",))

        def rcol_chain():
            P.mms([lambda e: e.transpose(PS[0:1, 0:128], mxc[:, 1:2], idf[:, :])],
                  reads=("mxcc", "idf"), writes=("PS.0",))
            P.op("vector", lambda e: e.tensor_copy(mrow[0:1, 224:352], PS[0:1, 0:128]),
                 reads=("PS.0",), writes=("mrow",))
            P.op("vector", lambda e: e.tensor_reduce(
                mrow[0:1, 352:353], mrow[0:1, 224:352], AX, AL.max),
                 reads=("mrow",), writes=("Mg",))
            P.op("vector", lambda e: e.tensor_scalar(
                mrow[0:1, 353:354], mrow[0:1, 352:353], 1.0 / 127.0, 1e-5, AL.mult, AL.max),
                 reads=("Mg",), writes=("xsc",))
            P.op("vector", lambda e: e.reciprocal(mrow[0:1, 354:355], mrow[0:1, 353:354]),
                 reads=("xsc",), writes=("xsr",))
            P.mms([lambda e: e.matmul(PS[:, 128:129], mrow[0:1, 96:224],
                                      mrow[0:1, 354:355], start=True, stop=True),
                   lambda e: e.matmul(PS[:, 129:130], mrow[0:1, 96:224],
                                      mrow[0:1, 353:354], start=True, stop=True)],
                  reads=("onesr", "xsr", "xsc"), writes=("PS.0",))
            P.op("vector", lambda e: e.tensor_copy(rcol[:, 0:2], PS[:, 128:130]),
                 reads=("PS.0",), writes=("rcol",))

        def hgate():
            STGKEYS = ("stg0_0", "stg0_1", "stg1_0", "stg1_1", "stg2_0", "stg2_1")
            P.op("gpsimd", lambda e: e.dma_start(out=stg[0:64, 0:DIN], in_=ag_out[0:64, :]),
                 reads=("ag_out",) + STGKEYS, writes=("hbs",), dma=True)
            P.op("vector", lambda e: e.tensor_scalar(
                ht[0:64, :], stg[0:64, 0:DIN], gats[0:64, :], None, AL.mult),
                 reads=("hbs", "gats", "ht"), writes=("ht",))

        def wqs(n):
            ws = n % 2
            ld(w8i[:, ws * 1024:(ws + 1) * 1024], Wq8[n * 128:(n + 1) * 128, :], f"w8i{ws}")
            P.op("vector", lambda e, ws=ws: e.tensor_copy(
                pool[:, 1024:2048], w8i[:, ws * 1024:(ws + 1) * 1024]),
                 reads=(f"w8i{ws}",), writes=("slz1",))
            P.op("vector", lambda e, n=n: e.tensor_scalar(
                xbcb[:, 0:1024], pool[:, 1024:2048], wscs[:, n:n + 1], rcol[:, 1:2],
                AL.mult, AL.mult),
                 reads=("slz1", "wscs", "rcol"), writes=("xbcb",))
            fns = []
            for j in range(8):
                fns.append(lambda e, j=j: e.transpose(
                    PT[:, j * 128:(j + 1) * 128], xbcb[:, j * 128:(j + 1) * 128], idb[:, :]))
            P.mms(fns, reads=("xbcb", "idb"), writes=("PT.0",))
            def evq(e, n=n):
                dst = wqtb[:, :].rearrange("p (k n2) -> p k n2", k=8)[:, :, n * 128:(n + 1) * 128]
                return e.tensor_copy(dst, PT[:, :].rearrange("p (k n2) -> p k n2", k=8))
            P.op("vector", evq, reads=("PT.0",), writes=("wqtb",))

        for db in range(16):
            s2 = db % 2
            wsl, wk_ = WSLT[s2]
            csl, ck_ = CSLT[s2]
            msl, mk_ = MSLT[s2]
            esl, ek_ = ESLT[s2]
            ld(wsl, WoT[db * 128:(db + 1) * 128, :], wk_)
            P.op("vector", lambda e, csl=csl, wsl=wsl: e.tensor_copy(csl, wsl),
                 reads=wk_, writes=ck_)
            fns = []
            for j in range(8):
                fns.append(lambda e, j=j, csl=csl: e.transpose(
                    PT[:, j * 128:(j + 1) * 128], csl[:, j * 128:(j + 1) * 128], idb[:, :]))
            P.mms(fns, reads=ck_ + ("idb",), writes=("PT.0",))
            P.op("vector", lambda e, msl=msl: e.tensor_copy(msl, PT[:, :]),
                 reads=("PT.0",), writes=mk_)
            pb = PA if s2 == 0 else PB
            pks = ("PA.0", "PA.1") if s2 == 0 else ("PB.0", "PB.1")
            fns = []
            for m in range(8):
                for j in range(2):
                    fns.append(lambda e, m=m, j=j, msl=msl, pb=pb: e.matmul(
                        pb[:, j * 512:(j + 1) * 512], msl[:, m * 128:(m + 1) * 128],
                        wktb[:, m * 1024 + j * 512: m * 1024 + (j + 1) * 512],
                        start=(m == 0), stop=(m == 7)))
            P.mms(fns, reads=mk_ + tuple(f"wkb{m}" for m in range(8)), writes=pks)
            P.op("vector", lambda e, db=db, esl=esl, pb=pb: e.tensor_scalar(
                esl, pb[:, :], nws[:, db:db + 1], None, AL.mult),
                 reads=pks + ("nws",), writes=ek_)
            P.op("scalar", lambda e, db=db, esl=esl: e.dma_start(
                out=wf_scr[db * 128:(db + 1) * 128, :], in_=esl),
                 reads=ek_, writes=("wf_scr",), dma=True)
            if db % 2 == 0:
                zreload(db // 2)
            if db == 5:
                rcol_chain()
            elif db == 7:
                hgate()
            elif db >= 8:
                wqs(db - 8)

        actx.close()
        xqs = sb("xqs", 8 * TT, BF16)
        pool = sb("pool", 2560)
        gwv = sb("gwv", 2 * 8 * 192, BF16)
        ybs = sb("ybs", DIN, BF16)
        ybT = sb("ybT", 2 * 2048, BF16)
        hbf = sb("hbf", 1024, BF16)
        dxs = sb("dxs", DIN, BF16)
        p2e = sb("p2e", 1024, BF16)
        zws = sb("zws", 1024)
        gpm = sb("gpm", 256)
        ABRIDGE = ("xg", "xnt", "cvt0", "cacc", "bn6", "bn2", "lnv", "rsv",
                   "mxg") + XBCXKEYS

        SZK1 = ("slz1",)

        def z_half(i, qq, half):
            szap = wstg[:, 0:2048] if qq == 0 else pool[:, 512:2560]
            szk = ("wstg",) if qq == 0 else SZK1
            fns = []
            for d in range(8):
                for j in range(2):
                    fns.append(lambda e, qq=qq, half=half, d=d, j=j: e.matmul(
                        PC[:, j * 512:(j + 1) * 512],
                        xnb[:, d * TW + 3 + qq * 128: d * TW + 3 + (qq + 1) * 128],
                        wintb[:, d * 2208 + half * 1024 + j * 512:
                              d * 2208 + half * 1024 + (j + 1) * 512],
                        start=(d == 0), stop=(d == 7)))
            P.mms(fns, reads=XNBKEYS + WINKEYS, writes=("PC.0", "PC.1"))
            P.op("scalar", lambda e: e.activation(
                zws[:, :], PC[:, :], AF.Sigmoid),
                 reads=("PC.0", "PC.1"), writes=("zws",))
            P.op("vector", lambda e, half=half, szap=szap: e.tensor_tensor(
                szap[:, half * 1024:(half + 1) * 1024],
                PC[:, :], zws[:, :], AL.mult),
                 reads=("PC.0", "PC.1", "zws"), writes=szk)

        def z_proj(i, qq):
            z_half(i, qq, 0)
            z_half(i, qq, 1)

        ybanks = ("PA.0", "PA.1", "PB.0", "PB.1")

        def scans_c(i, qq):
            chunk_small(qq, "C")
            bc_cast(qq, "C")
            bo = qq * 320
            P.mms([lambda e, bo=bo: e.matmul(PS[:, 256:384], bcb[0:64, bo:bo + Q],
                                             bcb[0:64, bo + Q:bo + 2 * Q],
                                             start=True, stop=True)],
                  reads=(f"Bb{qq}", f"Cb{qq}"), writes=("PS.0",))
            P.op("scalar", lambda e, qq=qq: e.copy(gpm[:, qq * 128:(qq + 1) * 128],
                                                   PS[:, 256:384]),
                 reads=("PS.0",), writes=(f"gpm{qq}",))

        def wbuild(i, qq):
            XTq = XT[:, qq * DIN:(qq + 1) * DIN]
            xtk = f"XT{qq}"
            SLK = (f"sHL{qq}_0", f"sHL{qq}_1",
                   f"LHL{qq}_0", f"LHL{qq}_1")
            sQ = sHL[0:64, qq * Q:(qq + 1) * Q]
            lQ = LHL[0:64, qq * Q:(qq + 1) * Q]
            def intra_g(g):
                gs = g % 2
                pb = PA if g < 2 else PB
                bank = ybanks[g]
                fns = []
                for j in range(8):
                    h = g * 8 + j
                    off = (h % 16) * 64
                    fns.append(lambda e, h=h, j=j, gs=gs, pb=pb, off=off, XTq=XTq: e.matmul(
                        pb[:, off:off + 64],
                        gwv[:, gs * 1536 + j * 192: gs * 1536 + j * 192 + 128],
                        XTq[:, h * 64:(h + 1) * 64], start=(j == 0), stop=False))
                P.mms(fns, reads=(f"gwvA{gs}0", f"gwvA{gs}1", xtk), writes=(bank,))

            for g in range(4):
                gs = g % 2
                for hb in range(2):
                    pck = "PC.0" if hb == 0 else "PC.1"
                    fns = []
                    for j in range(4):
                        h = g * 8 + hb * 4 + j
                        fns.append(lambda e, h=h, hb=hb, j=j, sQ=sQ: e.matmul(
                            PC[:, (hb * 4 + j) * 128:(hb * 4 + j + 1) * 128],
                            oht[0:64, h * 128:(h + 1) * 128], sQ,
                            start=(j == 0), stop=False))
                    fns.append(lambda e, g=g, hb=hb, lQ=lQ: e.matmul(
                        PC[:, hb * 512:(hb + 1) * 512], lQ,
                        oht[0:64, (g * 8 + hb * 4) * 128:(g * 8 + hb * 4 + 4) * 128],
                        start=False, stop=False))
                    fns.append(lambda e, hb=hb: e.matmul(
                        PC[:, hb * 512:(hb + 1) * 512], nbi[:, :],
                        rep_t(slo[:, :], 4), start=False, stop=True))
                    P.mms(fns, reads=SLK + ("oht", "nbi", "slo"), writes=(pck,))
                    hk = "hbfA" if hb == 0 else "hbfB"
                    P.op("scalar", lambda e, hb=hb: e.activation(
                        hbf[:, hb * 512:(hb + 1) * 512],
                        PC[:, hb * 512:(hb + 1) * 512], AF.Exp),
                         reads=(pck,), writes=(hk,))
                    def gwm(e, qq=qq, gs=gs, hb=hb):
                        dst = gwv[:, gs * 1536 + hb * 768: gs * 1536 + hb * 768 + 768] \
                            .rearrange("p (h c) -> p h c", h=4)[:, :, 0:128]
                        return e.tensor_tensor(
                            dst,
                            hbf[:, hb * 512:(hb + 1) * 512].rearrange("p (h c) -> p h c", h=4),
                            rep_t(gpm[:, qq * 128:(qq + 1) * 128], 4), AL.mult)
                    P.op("vector", gwm, reads=(hk, f"gpm{qq}"), writes=(f"gwvA{gs}{hb}",))
                if g >= 1:
                    intra_g(g - 1)
            intra_g(3)
            P.op("gpsimd", lambda e, XTq=XTq: e.tensor_tensor(
                dxs[:, :], XTq, rep_h(dpcb[:, :], 32, 64), AL.mult),
                 reads=(xtk, "dpcb"), writes=("dxs",))
            fns = []
            for j in range(4):
                pb = PA if j < 2 else PB
                fns.append(lambda e, j=j, pb=pb: e.matmul(
                    pb[:, (j % 2) * 512:(j % 2 + 1) * 512], idb[:, :],
                    dxs[:, j * 512:(j + 1) * 512], start=False, stop=False))
            P.mms(fns, reads=("dxs", "idb"), writes=ybanks)
            P.op("scalar", lambda e: e.copy(htb[0:64, :], ht[0:64, :]),
                 reads=("ht",), writes=("htb",))
            P.op("vector", lambda e, qq=qq: e.tensor_tensor(
                ht[0:64, :], ht[0:64, :],
                rep_h(aB[0:64, qq * 32:qq * 32 + 32], 32, 64), AL.mult),
                 reads=("ht", f"aB{qq}"), writes=("ht",))
            for half in range(2):
                fns = []
                for j in range(2):
                    fns.append(lambda e, qq=qq, half=half, j=j: e.matmul(
                        PC[:, j * 512:(j + 1) * 512],
                        bcb[0:64, qq * 320 + Q: qq * 320 + 2 * Q],
                        htb[0:64, half * 1024 + j * 512: half * 1024 + (j + 1) * 512],
                        start=True, stop=True))
                P.mms(fns, reads=(f"Cb{qq}", "htb"), writes=("PC.0", "PC.1"))
                def p2ev(e, qq=qq, half=half):
                    src1 = esT[:, qq * 32 + half * 16: qq * 32 + (half + 1) * 16] \
                        .unsqueeze(2).broadcast_to([128, 16, 64])
                    return e.tensor_tensor(p2e[:, :], PC[:, :], src1, AL.mult)
                P.op("vector", p2ev, reads=("PC.0", "PC.1", f"esT{qq}"), writes=("p2e",))
                fns = []
                for j in range(2):
                    pb = PA if half == 0 else PB
                    fns.append(lambda e, half=half, j=j, pb=pb: e.matmul(
                        pb[:, j * 512:(j + 1) * 512], idb[:, :],
                        p2e[:, j * 512:(j + 1) * 512], start=False, stop=True))
                P.mms(fns, reads=("p2e", "idb"), writes=ybanks)
            xw_and_contrib_C(qq)

        def glue_c(i, qq):
            szap = wstg[:, 0:2048] if qq == 0 else pool[:, 512:2560]
            szk = ("wstg",) if qq == 0 else SZK1
            for half in range(2):
                pb = PA if half == 0 else PB
                P.op("vector", lambda e, half=half, pb=pb, szap=szap: e.tensor_tensor(
                    ybs[:, half * 1024:(half + 1) * 1024],
                    pb[:, :], szap[:, half * 1024:(half + 1) * 1024], AL.mult),
                     reads=ybanks + szk, writes=("ybs",))
            for b4 in range(4):
                P.op("vector", lambda e, b4=b4: e.bn_stats(
                    dsm[0:128, 1524 + b4 * 6:1530 + b4 * 6],
                    ybs[:, b4 * 512:(b4 + 1) * 512]),
                     reads=("ybs",), writes=("gbn6",))
            P.op("vector", lambda e: e.bn_aggr(dsm[0:128, 1550:1552],
                                               dsm[0:128, 1524:1548]),
                 reads=("gbn6",), writes=("gbn2",))
            P.op("vector", lambda e: e.tensor_tensor(
                ssq[:, 0:1], dsm[0:128, 1550:1551], dsm[0:128, 1550:1551], AL.mult),
                 reads=("gbn2",), writes=("gms",))
            P.op("vector", lambda e: e.tensor_add(
                ssq[:, 1:2], ssq[:, 0:1], dsm[0:128, 1551:1552]),
                 reads=("gms", "gbn2"), writes=("ssqc",))
            P.op("scalar", lambda e: e.activation(ssq[:, 1:2], ssq[:, 1:2], AF.Ln,
                                                  bias=EPS),
                 reads=("ssqc",), writes=("lnss",))
            P.op("scalar", lambda e: e.activation(ssq[:, 1:2], ssq[:, 1:2], AF.Exp, scale=-0.5),
                 reads=("lnss",), writes=("rs",))
            P.op("vector", lambda e: e.tensor_scalar(
                ybs[:, 0:1024], ybs[:, 0:1024], ssq[:, 1:2], None, AL.mult),
                 reads=("ybs", "rs"), writes=("ybs",))
            P.op("scalar", lambda e: e.activation(
                ybs[:, 1024:2048], ybs[:, 1024:2048], AF.Copy, scale=ssq[:, 1:2]),
                 reads=("ybs", "rs"), writes=("ybs",))

        def ybt_c(qq):
            for hb in range(2):
                fns = []
                for j in range(8):
                    c = hb * 8 + j
                    fns.append(lambda e, c=c, j=j: e.transpose(
                        PT[:, j * 128:(j + 1) * 128],
                        ybs[:, c * 128:(c + 1) * 128], idb[:, :]))
                P.mms(fns, reads=("ybs", "idb"), writes=("PT.0",))
                def ybev(e, qq=qq, hb=hb):
                    dst = ybT[:, qq * 2048 + hb * 1024: qq * 2048 + (hb + 1) * 1024]
                    return e.tensor_copy(dst, PT[:, :]) if hb == 0 else e.copy(dst, PT[:, :])
                P.op("vector" if hb == 0 else "scalar", ybev,
                     reads=("PT.0",), writes=(f"ybT{qq}",))

        def xw_and_contrib_C(qq):
            XTq = XT[:, qq * DIN:(qq + 1) * DIN]
            xtk = f"XT{qq}"
            P.op("gpsimd", lambda e, qq=qq, XTq=XTq: e.tensor_tensor(
                Xw[:, :], XTq, rep_h(wTb[:, qq * 32:qq * 32 + 32], 32, 64), AL.mult),
                 reads=(xtk, f"wTb{qq}"), writes=("Xw",))
            for half in range(2):
                fns = []
                for j in range(2):
                    fns.append(lambda e, qq=qq, half=half, j=j: e.matmul(
                        PC[0:64, j * 512:(j + 1) * 512],
                        bcb[:, qq * 320 + 2 * Q: qq * 320 + 2 * Q + 64],
                        Xw[:, half * 1024 + j * 512: half * 1024 + (j + 1) * 512],
                        start=True, stop=True))
                P.mms(fns, reads=(f"BTb{qq}", "Xw"), writes=("PC.0", "PC.1"))
                P.op("vector", lambda e, half=half: e.tensor_add(
                    ht[0:64, half * 1024:(half + 1) * 1024],
                    ht[0:64, half * 1024:(half + 1) * 1024], PC[0:64, 0:1024]),
                     reads=("ht", "PC.0", "PC.1"), writes=("ht",))

        def head_c(i):
            def relo(e, i=i):
                src = xnf_scr.rearrange("(d p) t -> p d t", d=8)[:, :, i * 256:i * 256 + TW]
                dst = stg[:, 0:8 * TW].rearrange("p (d t) -> p d t", d=8)
                return e.dma_start(out=dst, in_=src)
            extra = ABRIDGE if i == 0 else ()
            P.op("sync", relo, writes=("stg0_0", "stg0_1", "stg1_0", "stg1_1",
                                       "stg2_0", "stg2_1", "hbs", "xnfre") + extra, dma=True)
            P.op("sync", lambda e, i=i: e.dma_start(
                out=Bt[0:64, :], in_=bc_scr[0:64, i * TT:(i + 1) * TT]),
                 writes=("Bt",), dma=True)
            P.op("sync", lambda e, i=i: e.dma_start(
                out=Ct[0:64, :], in_=bc_scr[64:128, i * TT:(i + 1) * TT]),
                 writes=("Ct",), dma=True)
            P.op("sync", lambda e, i=i: e.dma_start(
                out=dtt[0:32, :], in_=dt_scr[:, i * TT:(i + 1) * TT]),
                 writes=("dtt",), dma=True)
            P.op("sync", lambda e, i=i: e.dma_start(
                out=XT[:, :].rearrange("p (g c) -> p g c", g=2),
                in_=xt_scr[i * TT:(i + 1) * TT, :].rearrange("(g p) c -> p g c", g=2)),
                 writes=("XT0", "XT1"), dma=True)
            for d in range(8):
                def xc(e, d=d):
                    dst = xnb[:, d * TW:(d + 1) * TW]
                    s_ = stg[:, d * TW:(d + 1) * TW]
                    return e.tensor_copy(dst, s_) if d % 2 == 0 else e.copy(dst, s_)
                P.op("vector" if d % 2 == 0 else "scalar", xc,
                     reads=("xnfre",), writes=(f"xnb{d}",))

        def quant_c(i):
            for d in range(8):
                P.op("scalar", lambda e, d=d: e.activation(
                    pool[:, 0:TT], stg[:, d * TW + 3:(d + 1) * TW], AF.Copy,
                    scale=rcol[:, 0:1], bias=12582912.0),
                     reads=("xnfre", "rcol", "cpool"), writes=("qt",))
                P.op("scalar", lambda e, d=d: e.activation(
                    xqs[:, d * TT:(d + 1) * TT], pool[:, 0:TT], AF.Copy,
                    bias=-12582912.0),
                     reads=("qt", "cpool"), writes=(f"xqs{d}",))
            P.op("gpsimd", lambda e: e.dma_start(out=xq_scr, in_=xqs[:, :]),
                 reads=tuple(f"xqs{d}" for d in range(8)), writes=("xqscr",), dma=True)
            P.op("gpsimd", lambda e: e.dma_start(out=xqs[:, :], in_=xq_scr),
                 reads=("xqscr",), writes=tuple(f"xqs{d}" for d in range(8)), dma=True)

        def tail_c(i, nxt):
            for y in range(16):
                sl = y % 8
                P.op("sync", lambda e, y=y, sl=sl: e.dma_start(
                    out=wktb[:, sl * 1024:(sl + 1) * 1024],
                    in_=wf_scr[y * 128:(y + 1) * 128, :]),
                     writes=(f"wkb{sl}",), dma=True)
                for qq in range(2):
                    pb = PA if qq == 0 else PB
                    pks = ("PA.0", "PA.1") if qq == 0 else ("PB.0", "PB.1")
                    fns = []
                    for j in range(2):
                        fns.append(lambda e, y=y, qq=qq, sl=sl, j=j, pb=pb: e.matmul(
                            pb[:, j * 512:(j + 1) * 512],
                            ybT[:, qq * 2048 + y * 128: qq * 2048 + (y + 1) * 128],
                            wktb[:, sl * 1024 + j * 512: sl * 1024 + (j + 1) * 512],
                            start=(y == 0), stop=False))
                    P.mms(fns, reads=(f"wkb{sl}", f"ybT{qq}"), writes=pks)
                if nxt is not None:
                    if y == 1:
                        head_c(nxt)
                    elif y == 4:
                        z_half(nxt, 0, 0)
                    elif y == 7:
                        z_half(nxt, 0, 1)
                    elif y == 10:
                        z_half(nxt, 1, 0)
                    elif y == 13:
                        z_half(nxt, 1, 1)
            for qq in range(2):
                pb = PA if qq == 0 else PB
                pks = ("PA.0", "PA.1") if qq == 0 else ("PB.0", "PB.1")
                fns = []
                for k in range(8):
                    for j in range(2):
                        fns.append(lambda e, qq=qq, k=k, j=j, pb=pb: e.matmul(
                            pb[:, j * 512:(j + 1) * 512],
                            xqs[:, k * TT + qq * 128: k * TT + (qq + 1) * 128],
                            wqtb[:, k * 1024 + j * 512: k * 1024 + (j + 1) * 512],
                            start=False, stop=False))
                for j in range(2):
                    fns.append(lambda e, j=j, pb=pb: e.matmul(
                        pb[:, j * 512:(j + 1) * 512], oneb[0:1, :],
                        fbb[0:1, j * 512:(j + 1) * 512], start=False, stop=True))
                P.mms(fns, reads=("wqtb", "fbb", "oneb") +
                      tuple(f"xqs{k}" for k in range(8)), writes=pks)
                P.op("sync", lambda e, i=i, qq=qq: e.dma_start(
                    out=resb[:, 0:1024],
                    in_=Xc[3 + i * TT + qq * 128: 3 + i * TT + (qq + 1) * 128, :]),
                     writes=("xout",), dma=True)
                P.op("vector", lambda e, pb=pb: e.tensor_add(
                    resb[:, 1024:2048], pb[:, 0:1024], resb[:, 0:1024]),
                     reads=pks + ("xout",), writes=("osb",))
                P.op("scalar", lambda e, i=i, qq=qq: e.dma_start(
                    out=OUT[i * TT + qq * 128: i * TT + (qq + 1) * 128, :],
                    in_=resb[:, 1024:2048]),
                     reads=("osb",), writes=("outd",), dma=True)
            if nxt is not None:
                scans_c(nxt, 0)

        head_c(0)
        z_proj(0, 0)
        z_proj(0, 1)
        scans_c(0, 0)
        quant_c(0)
        for i in range(c_tiles):
            wbuild(i, 0)
            if i > 0:
                quant_c(i)
            scans_c(i, 1)
            glue_c(i, 0)
            wbuild(i, 1)
            ybt_c(0)
            glue_c(i, 1)
            ybt_c(1)
            tail_c(i, i + 1 if i + 1 < c_tiles else None)

        def dump(name, src, reads):
            if name in dbg_outs:
                P.op("gpsimd", lambda e, d=dbg_outs[name], s=src: e.dma_start(out=d, in_=s),
                     reads=reads, writes=(name,), dma=True)
        dump("dbg_xnf", xnf_scr, ("xnfg16",))
        dump("dbg_xbcx", xbcx_scr, ("xbspl",))
        dump("dbg_bc", bc_scr, ("bspl", "cspl"))
        dump("dbg_dt", dt_scr, ("dtspl",))
        dump("dbg_ht", ht[0:64, :], ("ht",))
        dump("dbg_xt", xt_scr, ("xtspl",))
        dump("dbg_siluz", wstg[:, 0:2048], ("wstg",))
        dump("dbg_yb", ybs[:, :], ("ybs",))
        dump("dbg_xqs", xqs[:, :], tuple(f"xqs{d}" for d in range(8)))
        dump("dbg_rcol", rcol[:, :], ("rcol",))
        dump("dbg_xw", Xw[:, :], ("Xw",))
        dump("dbg_wts", wTs[:, :], ("wTs",))
        dump("dbg_ab", aB[:, :], ("aB",))
        dump("dbg_dsm", dsm[:, :], ("css", "csL", "csw", "csa"))
        dump("dbg_bcb", bcb[:, :], ("BTb",))
        if dbgpc is not None:
            dump("dbg_pc", dbgpc[:, :], ("dbgpc",))

        P.final_barrier("gpsimd")

        with nc.Block() as block:
            @block.sync
            def _(e):
                for f in P.steps["sync"]:
                    f(e)

            @block.vector
            def _(e):
                for f in P.steps["vector"]:
                    f(e)

            @block.scalar
            def _(e):
                for f in P.steps["scalar"]:
                    f(e)

            @block.tensor
            def _(e):
                for f in P.steps["tensor"]:
                    f(e)

            @block.gpsimd
            def _(e):
                for f in P.steps["gpsimd"]:
                    f(e)

    return nc



def make_inputs(L0_attn_X, L0_attn_in_proj_w, L0_attn_conv_w, L0_attn_conv_b,
                L0_attn_dt_bias, L0_attn_A_log, L0_attn_D, L0_attn_norm_w,
                L0_attn_out_proj_w, L0_attn_to_h_w, L0_ffn_weight_int8,
                L0_ffn_weight_scale, L0_ffn_bias):
    X = np.asarray(L0_attn_X)
    WinT = np.ascontiguousarray(np.asarray(L0_attn_in_proj_w).T)
    WoT = np.ascontiguousarray(np.asarray(L0_attn_out_proj_w).T)
    WkT = np.ascontiguousarray(np.asarray(L0_attn_to_h_w).T[:, 0:DIM])
    Wq8 = np.ascontiguousarray(np.asarray(L0_ffn_weight_int8))
    cw = np.asarray(L0_attn_conv_w).reshape(2176, 4)
    cb = np.asarray(L0_attn_conv_b)
    cwx = np.ascontiguousarray(cw[:2048].reshape(16, 128, 4).transpose(1, 0, 2).reshape(128, 64))
    cbx = np.ascontiguousarray(cb[:2048].reshape(16, 128).T)
    cwB = np.zeros((128, 4), np.float32); cwB[0:64] = cw[2048:2112]
    cbB = np.zeros((128, 1), np.float32); cbB[0:64, 0] = cb[2048:2112]
    cwB[64:128] = cw[2112:2176]
    cbB[64:128, 0] = cb[2112:2176]
    dtb = np.zeros((128, 1), np.float32); dtb[0:32, 0] = np.asarray(L0_attn_dt_bias)
    alog = np.zeros((128, 1), np.float32); alog[0:32, 0] = np.asarray(L0_attn_A_log)
    D = np.asarray(L0_attn_D)
    dpc = np.ascontiguousarray(np.broadcast_to(D[None, :], (128, 32)))
    nwv = np.asarray(L0_attn_norm_w)
    nw = np.ascontiguousarray(nwv.reshape(16, 128).T)
    wsc = np.ascontiguousarray(np.asarray(L0_ffn_weight_scale).reshape(8, 128).T)
    fb = np.ascontiguousarray(np.broadcast_to(np.asarray(L0_ffn_bias)[None, :], (128, DIM)))

    r3 = (np.arange(96)[None, :] % 32 == np.arange(32)[:, None]).astype(np.float32)
    cvals = np.zeros((128, len(CONSTVALS)), np.float32)
    for ci, cv in enumerate((1e-5, 1.0, 0.0, -1.0, -0.5, 1.0 / 2048.0, 1e-30, 0.5)):
        cvals[:, ci] = cv
    ident = np.eye(128, dtype=np.float32)
    IDb = ident
    NBI = -16384.0 * ident
    SLO = (np.arange(128)[:, None] > np.arange(128)[None, :]).astype(np.float32)
    kk = np.arange(96) % 32
    hh = np.arange(4096) // 128
    OHTv = np.zeros((128, 4096), np.float32)
    OHTv[0:96] = (kk[:, None] == hh[None, :]).astype(np.float32)
    D96v = OHTv.copy()
    ONEBv = np.ones((128, 128), np.float32)

    import ml_dtypes
    bf = lambda a: np.ascontiguousarray(a).astype(ml_dtypes.bfloat16)

    common = dict(WinT=WinT, WoT=WoT, WkT=WkT, Wq8=Wq8, wsc=wsc, fbias=fb, nw=nw,
                  cwx=cwx, cbx=cbx, cwB=cwB, cbB=cbB,
                  dtb=dtb, alog=alog, dpc=dpc,
                  IDb=bf(IDb), NBI=bf(NBI), SLO=bf(SLO), OHT=bf(OHTv),
                  ONEB=bf(ONEBv), IDf=ident, CVALS=cvals, R3F=r3)
    maps = []
    for c in range(8):
        b, h = c // 2, c % 2
        Xc = np.zeros((ST, DIM), np.float32)
        if h == 1:
            Xc[0:3] = X[b, T - 3:T, :]
        Xc[3:3 + T] = X[b, h * T:(h + 1) * T, :]
        g = np.zeros((128, 1), np.float32)
        g[:, 0] = float(h)
        m = dict(common)
        m["Xc"] = Xc
        m["gate"] = g
        maps.append(m)
    return maps


_NC_CACHE = {}


def kernel(**inputs):
    key = "full"
    if key not in _NC_CACHE:
        _NC_CACHE[key] = build_nc()
    nc = _NC_CACHE[key]
    maps = make_inputs(**inputs)
    res = run_bass_kernel_spmd(nc, maps, core_ids=list(range(8)))
    out = np.empty((4, 4096, DIM), np.float32)
    for c in range(8):
        b, h = c // 2, c % 2
        out[b, h * T:(h + 1) * T, :] = res.results[c]["OUT"]
    return out
